# version 01
# baseline (speedup 1.0000x reference)
import numpy as np
import ml_dtypes
from contextlib import ExitStack

import concourse.bass as bass
import concourse.mybir as mybir
from concourse.bass import IndirectOffsetOnAxis
from concourse.bass_utils import run_bass_kernel_spmd

F32 = mybir.dt.float32
BF16 = mybir.dt.bfloat16
I32 = mybir.dt.int32
Alu = mybir.AluOpType
Act = mybir.ActivationFunctionType
AX = mybir.AxisListType.X
BF = ml_dtypes.bfloat16

B, L, D = 4, 4096, 1024
NH, DK, DV = 4, 256, 256
E, CAP, FF = 8, 2048, 4096
NCORE = 8
SL = L // NCORE
NT = 4 * SL // 128
LN_EPS = 1e-5



class Prog:

    ENGINES = ("tensor", "vector", "scalar", "gpsimd", "sync")

    def __init__(self, nc, ctx):
        self.nc = nc
        self.ctx = ctx
        self.q = {e: [] for e in self.ENGINES}
        self.sems = {}
        self.cnt = {}

    def sem(self, name):
        if name not in self.sems:
            self.sems[name] = self.ctx.enter_context(self.nc.semaphore(name))
            self.cnt[name] = 0
        return self.sems[name]

    def wait(self, eng, name, val=None):
        s = self.sem(name)
        v = self.cnt[name] if val is None else val
        self.q[eng].append(lambda En, s=s, v=v: En.wait_ge(s, v))
        return v

    def op(self, eng, f, inc=None, k=1):
        if inc is not None:
            s = self.sem(inc)
            self.cnt[inc] += k

            def g(En, f=f, s=s, k=k):
                f(En).then_inc(s, k)

            self.q[eng].append(g)
            return self.cnt[inc]
        self.q[eng].append(lambda En, f=f: f(En))
        return None

    def dma(self, eng, out, in_, inc, **kw):
        return self.op(eng, lambda En: En.dma_start(out=out, in_=in_, **kw),
                       inc=inc, k=16)

    def tdma(self, out, in_, inc="ds"):
        return self.op("scalar", lambda En: En.dma_start_transpose(out, in_),
                       inc=inc, k=16)

    def finish(self):
        block = self.ctx.enter_context(self.nc.Block())
        for e in self.ENGINES:
            ops = self.q[e]

            def runner(En, ops=ops):
                for g in ops:
                    g(En)

            getattr(block, e)(runner)


def sb(ctx, nc, name, shape, dt):
    return ctx.enter_context(nc.sbuf_tensor(name, shape, dt))


def psum(ctx, nc, name, shape=(128, 512), dt=F32):
    return ctx.enter_context(nc.psum_tensor(name, list(shape), dt))


class LNS:

    def __init__(self, ctx, nc, tag, n=D):
        self.tmp = sb(ctx, nc, f"{tag}_tmp", [128, n], F32)
        self.sq = sb(ctx, nc, f"{tag}_sq", [128, n], F32)
        self.nm = sb(ctx, nc, f"{tag}_nm", [128, 1], F32)
        self.ss = sb(ctx, nc, f"{tag}_ss", [128, 1], F32)
        self.sd = sb(ctx, nc, f"{tag}_sd", [128, 1], F32)
        self.rs = sb(ctx, nc, f"{tag}_rs", [128, 1], F32)
        self.n = n


def ln_tile(P, xt, out, S, gb=None, out_bf=None):
    n = S.n
    P.op("vector", lambda En: En.reduce_sum(S.nm[:], xt, axis=AX), inc="v")
    P.wait("vector", "v")
    P.op("vector", lambda En: En.tensor_scalar_mul(S.nm[:], S.nm[:], -1.0 / n), inc="v")
    P.wait("vector", "v")
    P.op("vector", lambda En: En.tensor_scalar_add(S.tmp[:], xt, S.nm[:]), inc="v")
    P.wait("scalar", "v")
    P.op("scalar", lambda En: En.activation(S.sq[:], S.tmp[:], Act.Square,
                                            accum_out=S.ss[:]), inc="s")
    P.wait("vector", "s")
    P.op("vector", lambda En: En.tensor_scalar(S.ss[:], S.ss[:], 1.0 / n, LN_EPS,
                                               Alu.mult, Alu.add), inc="v")
    P.wait("scalar", "v")
    P.op("scalar", lambda En: En.activation(S.sd[:], S.ss[:], Act.Sqrt), inc="s")
    P.wait("vector", "s")
    P.op("vector", lambda En: En.reciprocal(S.rs[:], S.sd[:]), inc="v")
    P.wait("vector", "v")
    P.op("vector", lambda En: En.tensor_scalar_mul(out, S.tmp[:], S.rs[:]), inc="v")
    if gb is not None:
        g_rep, b_rep = gb
        P.op("vector", lambda En: En.tensor_mul(out, out, g_rep[:]), inc="v")
        P.op("vector", lambda En: En.tensor_add(out, out, b_rep[:]), inc="v")
    if out_bf is not None:
        P.op("vector", lambda En: En.tensor_copy(out_bf, out), inc="v")



def build_B(debug=False):
    nc = bass.Bass("TRN2", target_bir_lowering=False, debug=False)
    xs = nc.dram_tensor("xs", [4 * SL, D], F32, kind="ExternalInput").ap()
    wqkvT = nc.dram_tensor("wqkvT", [D, 3 * D], F32, kind="ExternalInput").ap()
    bqkv = nc.dram_tensor("bqkv", [1, 3 * D], F32, kind="ExternalInput").ap()
    woT = nc.dram_tensor("woT", [D, D], F32, kind="ExternalInput").ap()
    bo = nc.dram_tensor("bo", [1, D], F32, kind="ExternalInput").ap()
    ln1g = nc.dram_tensor("ln1g", [1, D], F32, kind="ExternalInput").ap()
    ln1b = nc.dram_tensor("ln1b", [1, D], F32, kind="ExternalInput").ap()
    wg = nc.dram_tensor("wg", [D, E], F32, kind="ExternalInput").ap()

    y1_out = nc.dram_tensor("y1", [4 * SL, D], F32, kind="ExternalOutput").ap()
    idx_out = nc.dram_tensor("idx", [128, NT], F32, kind="ExternalOutput").ap()
    gv_out = nc.dram_tensor("gv", [128, NT], F32, kind="ExternalOutput").ap()
    lr_out = nc.dram_tensor("lr", [128, NT], F32, kind="ExternalOutput").ap()
    cnt_out = nc.dram_tensor("cnt", [1, 4 * E], F32, kind="ExternalOutput").ap()
    if debug:
        dbg_xn = nc.dram_tensor("dbg_xn", [128, 8, 4 * SL], F32, kind="ExternalOutput").ap()
        dbg_m = nc.dram_tensor("dbg_m", [128, 8, NT, 128], F32, kind="ExternalOutput").ap()
        dbg_s = nc.dram_tensor("dbg_s", [128, 64], F32, kind="ExternalOutput").ap()
        dbg_lgt = nc.dram_tensor("dbg_lgt", [128, 8], F32, kind="ExternalOutput").ap()
        dbg_exp = nc.dram_tensor("dbg_exp", [128, 8], F32, kind="ExternalOutput").ap()
        dbg_oh = nc.dram_tensor("dbg_oh", [128, 8], F32, kind="ExternalOutput").ap()
        dbg_nm = nc.dram_tensor("dbg_nm", [128, 1], F32, kind="ExternalOutput").ap()
        dbg_a = nc.dram_tensor("dbg_a", [128, 64], F32, kind="ExternalOutput").ap()
        dbg_qkv = nc.dram_tensor("dbg_qkv", [128, 4, 3 * D], F32, kind="ExternalOutput").ap()
        dbg_mha = nc.dram_tensor("dbg_mha", [128, 8, NT, 128], F32, kind="ExternalOutput").ap()

    ones_col = nc.inline_tensor(np.ones((128, 1), BF), "c_ones_col").ap()
    triu_s = nc.inline_tensor(np.triu(np.ones((128, 128)), 1).astype(BF), "c_triu_s").ap()
    all1 = nc.inline_tensor(np.ones((128, 128), BF), "c_all1").ap()
    iota8 = nc.inline_tensor(np.tile(np.arange(8, dtype=np.float32), (128, 1)), "c_iota8").ap()
    ones8 = nc.inline_tensor(np.ones((128, 8), np.float32), "c_ones8").ap()

    with ExitStack() as ctx:
        P = Prog(nc, ctx)
        Y1FM = None
        xt = sb(ctx, nc, "xt", [128, D], F32)
        xnf = sb(ctx, nc, "xnf", [128, D], F32)
        S = LNS(ctx, nc, "lnB")
        M = sb(ctx, nc, "M", [128, 8, NT, 128], BF16)
        FMb = sb(ctx, nc, "FMb", [128, 8, 4 * SL], BF16)
        big = sb(ctx, nc, "big", [128, 8, 3 * D], BF16)
        wqkv_sb = big

        wo_sb = sb(ctx, nc, "wo_sb", [128, 8, D], BF16)
        wg_sb = sb(ctx, nc, "wg_sb", [128, 8, E], BF16)
        bqkv_sb = sb(ctx, nc, "bqkv_sb", [1, 3 * D], BF16)
        bo_sb = sb(ctx, nc, "bo_sb", [1, D], BF16)
        onescol_sb = sb(ctx, nc, "onescol_sb", [128, 1], BF16)
        trius_sb = sb(ctx, nc, "trius_sb", [128, 128], BF16)
        all1_sb = sb(ctx, nc, "all1_sb", [128, 128], BF16)
        iota8_sb = sb(ctx, nc, "iota8_sb", [128, 8], F32)
        ones8_sb = sb(ctx, nc, "ones8_sb", [128, 8], F32)
        qkv_sb = sb(ctx, nc, "qkv_sb", [128, 4, 3 * D], BF16)
        sblk = sb(ctx, nc, "sblk", [128, 64], F32)
        sblk2 = sb(ctx, nc, "sblk2", [128, 64], F32)
        smax = sb(ctx, nc, "smax", [128, 16], F32)
        ssum = sb(ctx, nc, "ssum", [128, 16], F32)
        ttr_junk = sb(ctx, nc, "ttr_junk", [128, 256], BF16)
        t0 = sb(ctx, nc, "t0", [128, 256], F32)
        t1 = sb(ctx, nc, "t1", [128, 256], F32)
        g_rep = sb(ctx, nc, "g_rep", [128, D], F32)
        b_rep = sb(ctx, nc, "b_rep", [128, D], F32)
        y1f = sb(ctx, nc, "y1f", [128, D], F32)
        oh = sb(ctx, nc, "oh", [128, 8], F32)
        ohf = sb(ctx, nc, "ohf", [128, NT, 8], F32)
        ohb = sb(ctx, nc, "ohb", [128, NT, 8], BF16)
        eqt = sb(ctx, nc, "eqt", [128, 8], F32)
        cum = sb(ctx, nc, "cum", [128, 8], F32)
        idx_sb = sb(ctx, nc, "idx_sb", [128, NT], F32)
        gv_sb = sb(ctx, nc, "gv_sb", [128, NT], F32)
        lr_sb = sb(ctx, nc, "lr_sb", [128, NT], F32)
        cnt_sb = sb(ctx, nc, "cnt_sb", [1, 4 * E], F32)
        lgt = sb(ctx, nc, "lgt", [128, 8], F32)

        Y1FM = big[:].rearrange("p a b -> p (a b)")[:, 0:8*2048*1].rearrange("p (k t) -> p k t", k=8)
        pb = [psum(ctx, nc, f"pb{i}") for i in range(8)]

        P.dma("sync", onescol_sb[:], ones_col, "dl")
        P.dma("sync", trius_sb[:], triu_s, "dl")
        P.dma("sync", all1_sb[:], all1, "dl")
        P.dma("sync", iota8_sb[:], iota8, "dl")
        P.dma("sync", ones8_sb[:], ones8, "dl")
        P.dma("sync", g_rep[:], ln1g[0:1, :].broadcast_to([128, D]), "dl")
        P.dma("sync", b_rep[:], ln1b[0:1, :].broadcast_to([128, D]), "dl")
        const_done = P.cnt["dl"]
        P.dma("gpsimd", wqkv_sb[:], wqkvT.rearrange("(k p) o -> p k o", p=128), "dg")
        P.dma("gpsimd", wo_sb[:], woT.rearrange("(k p) o -> p k o", p=128), "dg")
        P.dma("gpsimd", wg_sb[:], wg.rearrange("(k p) o -> p k o", p=128), "dg")
        P.dma("gpsimd", bqkv_sb[:], bqkv, "dg")
        P.dma("gpsimd", bo_sb[:], bo, "dg")
        w_done = P.cnt["dg"]

        for tt in range(NT):
            dv = P.dma("sync", xt[:], xs[tt * 128:(tt + 1) * 128, :], "dl")
            P.wait("vector", "dl", dv)
            P.wait("scalar", "dl", dv)
            ln_tile(P, xt[:], xnf[:], S, out_bf=M[:, :, tt, :])
            P.wait("sync", "s", P.cnt["s"])

        P.wait("scalar", "v", P.cnt["v"])
        P.wait("scalar", "dg", w_done)
        for k in range(8):
            P.tdma(FMb[:, k, :].rearrange("p (t c) -> p t c", t=NT),
                   M[:, k, :, :])
        tp_done = P.cnt["ds"]

        if debug:
            P.wait("gpsimd", "ds", tp_done)
            P.dma("gpsimd", dbg_xn, FMb[:], "dg")
            P.dma("gpsimd", dbg_m, M[:], "dg")
        P.wait("tensor", "ds", tp_done)
        P.wait("tensor", "dg", w_done)
        P.wait("tensor", "dl", const_done)
        for lt in range(4):
            for bb in range(4):
                tt = bb * 4 + lt
                for k in range(8):
                    for ob in range(6):
                        P.op("tensor", lambda En, k=k, ob=ob, tt=tt: En.matmul(
                            pb[ob][:, :],
                            FMb[:, k, tt * 128:(tt + 1) * 128],
                            wqkv_sb[:, k, ob * 512:(ob + 1) * 512],
                            start=(k == 0), stop=False))
                for ob in range(6):
                    P.op("tensor", lambda En, ob=ob: En.matmul(
                        pb[ob][:, :], all1_sb[0:1, :],
                        bqkv_sb[0:1, ob * 512:(ob + 1) * 512],
                        start=False, stop=True), inc="t")
                P.wait("scalar", "t")
                for ob in range(6):
                    P.op("scalar", lambda En, ob=ob, bb=bb: En.copy(
                        qkv_sb[:, bb, ob * 512:(ob + 1) * 512], pb[ob][:, :]), inc="s")
                P.wait("tensor", "s")
            evac_done = P.cnt["s"]
            if debug and lt == 0:
                P.wait("gpsimd", "s", evac_done)
                P.dma("gpsimd", dbg_qkv, qkv_sb[:], "dg")
            P.wait("vector", "s", evac_done)
            for h in range(NH):
                for i in range(4):
                    for j in range(4):
                        col = h * 16 + i * 4 + j
                        P.op("vector", lambda En, h=h, i=i, j=j, col=col:
                             En.scalar_tensor_tensor(
                                 ttr_junk[:],
                                 qkv_sb[:, i, h * 256:(h + 1) * 256],
                                 1.0 / 16.0,
                                 qkv_sb[:, j, D + h * 256:D + (h + 1) * 256],
                                 Alu.mult, Alu.mult,
                                 accum_out=sblk[:, col:col + 1]), inc="v")
            if debug and lt == 0:
                P.wait("sync", "v", P.cnt["v"])
                P.dma("sync", dbg_s, sblk[:], "do")
            P.op("vector", lambda En: En.reduce_max(
                smax[:], sblk[:].rearrange("p (a j) -> p a j", j=4), axis=AX), inc="v")
            P.wait("vector", "v")
            P.op("vector", lambda En: En.tensor_sub(
                sblk2[:].rearrange("p (a j) -> p a j", j=4),
                sblk[:].rearrange("p (a j) -> p a j", j=4),
                smax[:].unsqueeze(2).broadcast_to([128, 16, 4])), inc="v")
            P.wait("scalar", "v")
            P.op("scalar", lambda En: En.activation(sblk[:], sblk2[:], Act.Exp), inc="s")
            P.wait("vector", "s")
            P.op("vector", lambda En: En.reduce_sum(
                ssum[:], sblk[:].rearrange("p (a j) -> p a j", j=4), axis=AX), inc="v")
            P.wait("vector", "v")
            P.op("vector", lambda En: En.reciprocal(smax[:, 0:16], ssum[:]), inc="v")
            P.wait("vector", "v")
            P.op("vector", lambda En: En.tensor_mul(
                sblk2[:].rearrange("p (a j) -> p a j", j=4),
                sblk[:].rearrange("p (a j) -> p a j", j=4),
                smax[:].unsqueeze(2).broadcast_to([128, 16, 4])), inc="v")
            if debug and lt == 0:
                P.wait("sync", "v", P.cnt["v"])
                P.dma("sync", dbg_a, sblk2[:], "do")
                P.wait("vector", "do", P.cnt["do"])
            P.wait("vector", "v")
            for h in range(NH):
                for i in range(4):
                    tt = i * 4 + lt
                    base = h * 16 + i * 4
                    v0 = qkv_sb[:, 0, 2 * D + h * 256:2 * D + (h + 1) * 256]
                    v1_ = qkv_sb[:, 1, 2 * D + h * 256:2 * D + (h + 1) * 256]
                    v2 = qkv_sb[:, 2, 2 * D + h * 256:2 * D + (h + 1) * 256]
                    v3 = qkv_sb[:, 3, 2 * D + h * 256:2 * D + (h + 1) * 256]
                    P.op("vector", lambda En, v0=v0, base=base: En.tensor_scalar_mul(
                        t0[:], v0, sblk2[:, base:base + 1]), inc="v")
                    P.op("vector", lambda En, v1_=v1_, base=base: En.scalar_tensor_tensor(
                        t1[:], v1_, sblk2[:, base + 1:base + 2], t0[:],
                        Alu.mult, Alu.add), inc="v")
                    P.op("vector", lambda En, v2=v2, base=base: En.scalar_tensor_tensor(
                        t0[:], v2, sblk2[:, base + 2:base + 3], t1[:],
                        Alu.mult, Alu.add), inc="v")
                    P.op("vector", lambda En, v3=v3, base=base, tt=tt, h=h: En.scalar_tensor_tensor(
                        M[:, 2 * h:2 * h + 2, tt, :], v3,
                        sblk2[:, base + 3:base + 4], t0[:].rearrange("p (a c) -> p a c", a=2),
                        Alu.mult, Alu.add), inc="v")
            P.wait("tensor", "v")

        if debug:
            P.wait("gpsimd", "v", P.cnt["v"])
            P.dma("gpsimd", dbg_mha, M[:], "dg")
        P.wait("scalar", "v", P.cnt["v"])
        for k in range(8):
            P.tdma(FMb[:, k, :].rearrange("p (t c) -> p t c", t=NT),
                   M[:, k, :, :])
        tp2_done = P.cnt["ds"]

        P.wait("tensor", "ds", tp2_done)
        for tt in range(NT):
            for k in range(8):
                for ob in range(2):
                    P.op("tensor", lambda En, k=k, ob=ob, tt=tt: En.matmul(
                        pb[ob][:, :],
                        FMb[:, k, tt * 128:(tt + 1) * 128],
                        wo_sb[:, k, ob * 512:(ob + 1) * 512],
                        start=(k == 0), stop=False))
            for ob in range(2):
                P.op("tensor", lambda En, ob=ob: En.matmul(
                    pb[ob][:, :], all1_sb[0:1, :],
                    bo_sb[0:1, ob * 512:(ob + 1) * 512],
                    start=False, stop=True), inc="t")
            tdone = P.cnt["t"]
            dv = P.dma("sync", xt[:], xs[tt * 128:(tt + 1) * 128, :], "dl")
            P.wait("vector", "dl", dv)
            P.wait("scalar", "dl", dv)
            ln_tile(P, xt[:], xnf[:], S)
            P.wait("vector", "t", tdone)
            for ob in range(2):
                P.op("vector", lambda En, ob=ob: En.tensor_add(
                    xnf[:, ob * 512:(ob + 1) * 512], xnf[:, ob * 512:(ob + 1) * 512],
                    pb[ob][:, :]), inc="v")
            psum_free = P.cnt["v"]
            ln_tile(P, xnf[:], y1f[:], S, gb=(g_rep, b_rep), out_bf=M[:, :, tt, :])
            vdone = P.cnt["v"]
            P.wait("sync", "v", vdone)
            P.dma("sync", y1_out[tt * 128:(tt + 1) * 128, :], y1f[:], "do")
            P.wait("tensor", "v", psum_free)

        P.wait("scalar", "v", P.cnt["v"])
        for k in range(8):
            P.tdma(Y1FM[:, k, :].rearrange("p (t c) -> p t c", t=NT),
                   M[:, k, :, :])
        tp3_done = P.cnt["ds"]

        P.wait("tensor", "ds", tp3_done)
        for tt in range(NT):
            for k in range(8):
                P.op("tensor", lambda En, k=k, tt=tt: En.matmul(
                    pb[6][:, 0:8],
                    Y1FM[:, k, tt * 128:(tt + 1) * 128],
                    wg_sb[:, k, :],
                    start=(k == 0), stop=(k == 7)),
                    inc=("t" if k == 7 else None))
            P.wait("vector", "t")
            P.op("vector", lambda En: En.tensor_copy(lgt[:], pb[6][:, 0:8]), inc="v")
            P.wait("tensor", "v")
            P.wait("vector", "v")
            P.op("vector", lambda En: En.reduce_max(S.nm[:], lgt[:], axis=AX), inc="v")
            P.wait("vector", "v")
            P.op("vector", lambda En: En.tensor_sub(
                cum[:], lgt[:], S.nm[:].broadcast_to([128, 8])), inc="v")
            P.wait("scalar", "v")
            P.op("scalar", lambda En: En.activation(eqt[:], cum[:], Act.Exp), inc="s")
            P.wait("vector", "s")
            P.op("vector", lambda En: En.reduce_sum(S.ss[:], eqt[:], axis=AX), inc="v")
            P.wait("vector", "v")
            P.op("vector", lambda En, tt=tt: En.reciprocal(gv_sb[:, tt:tt + 1], S.ss[:]), inc="v")
            if debug and tt == 0:
                P.wait("sync", "v", P.cnt["v"])
                P.wait("sync", "s", P.cnt["s"])
                P.dma("sync", dbg_lgt, lgt[:], "do")
                P.dma("sync", dbg_exp, eqt[:], "do")
                P.dma("sync", dbg_nm, S.nm[:], "do")
            P.op("vector", lambda En: En.tensor_single_scalar(oh[:], eqt[:], 1.0,
                                                              Alu.is_ge), inc="v")
            P.wait("vector", "v")
            P.op("vector", lambda En: En.tensor_tensor_scan(
                cum[:], ones8_sb[:], oh[:], 0.0, Alu.mult, Alu.add), inc="v")
            P.wait("vector", "v")
            P.op("vector", lambda En: En.tensor_single_scalar(cum[:], cum[:], 1.0,
                                                              Alu.is_le), inc="v")
            P.op("vector", lambda En: En.tensor_mul(oh[:], oh[:], cum[:]), inc="v")
            if debug and tt == 0:
                P.wait("sync", "v", P.cnt["v"])
                P.dma("sync", dbg_oh, oh[:], "do")
                P.wait("vector", "do", P.cnt["do"])
            P.op("vector", lambda En: En.tensor_mul(eqt[:], oh[:], iota8_sb[:]), inc="v")
            P.wait("vector", "v")
            P.op("vector", lambda En, tt=tt: En.reduce_sum(
                idx_sb[:, tt:tt + 1], eqt[:], axis=AX), inc="v")
            P.op("vector", lambda En, tt=tt: En.tensor_copy(ohf[:, tt, :], oh[:]), inc="v")
            P.op("vector", lambda En, tt=tt: En.tensor_copy(ohb[:, tt, :], oh[:]), inc="v")
        ohb_done = P.cnt["v"]

        P.wait("tensor", "v", ohb_done)
        for bb in range(4):
            for lt in range(4):
                tt = bb * 4 + lt
                for lt2 in range(lt + 1):
                    tt2 = bb * 4 + lt2
                    mask = trius_sb if lt2 == lt else all1_sb
                    P.op("tensor", lambda En, tt2=tt2, mask=mask, lt2=lt2, lt=lt:
                         En.matmul(pb[7][:, 0:8], mask[:], ohb[:, tt2, :],
                                   start=(lt2 == 0), stop=(lt2 == lt)),
                         inc=("t" if lt2 == lt else None))
                P.wait("vector", "t")
                P.op("vector", lambda En, tt=tt: En.tensor_mul(
                    eqt[:], ohf[:, tt, :], pb[7][:, 0:8]), inc="v")
                P.wait("vector", "v")
                P.op("vector", lambda En, tt=tt: En.reduce_sum(
                    lr_sb[:, tt:tt + 1], eqt[:], axis=AX), inc="v")
                P.wait("tensor", "v")
            for lt in range(4):
                tt = bb * 4 + lt
                P.op("tensor", lambda En, tt=tt, lt=lt: En.matmul(
                    pb[7][0:1, 0:8], onescol_sb[:, 0:1], ohb[:, tt, :],
                    start=(lt == 0), stop=(lt == 3)),
                    inc=("t" if lt == 3 else None))
            P.wait("vector", "t")
            P.op("vector", lambda En, bb=bb: En.tensor_copy(
                cnt_sb[0:1, bb * 8:(bb + 1) * 8], pb[7][0:1, 0:8]), inc="v")
            P.wait("tensor", "v")

        P.wait("sync", "v", P.cnt["v"])
        P.dma("sync", idx_out, idx_sb[:], "do")
        P.dma("sync", gv_out, gv_sb[:], "do")
        P.dma("sync", lr_out, lr_sb[:], "do")
        P.dma("sync", cnt_out, cnt_sb[:], "do")
        P.wait("sync", "do", P.cnt["do"])

        P.finish()
    return nc




def emit_slots(P, nc, ctx, tag, cnt_all, msel, idx_pt, lr_pt, posc, coreoff,
               iota8_sb, want_tokid, poff=None):
    mm_mask = np.zeros((32, 32), np.float32)
    for c1 in range(8):
        for b1 in range(4):
            for c2 in range(8):
                for b2 in range(4):
                    if (b1 < b2) or (b1 == b2 and c1 < c2):
                        mm_mask[c1 * 4 + b1, c2 * 4 + b2] = 1.0
    mmc = nc.inline_tensor(mm_mask, f"c_offmask_{tag}").ap()

    cnt_sb = sb(ctx, nc, f"{tag}_cnt", [32, 8], F32)
    mmc_sb = sb(ctx, nc, f"{tag}_mmc", [32, 32], F32)
    msel_sb = sb(ctx, nc, f"{tag}_msel", [32, 4], F32)
    off_sb = sb(ctx, nc, f"{tag}_off", [32, 8], F32)
    offmy_sb = sb(ctx, nc, f"{tag}_offmy", [4, 8], F32)
    offrep = sb(ctx, nc, f"{tag}_offrep", [128, 4, 8], F32)
    idx_sb = sb(ctx, nc, f"{tag}_idx", [128, NT], F32)
    lr_sb = sb(ctx, nc, f"{tag}_lr", [128, NT], F32)
    posc_sb = sb(ctx, nc, f"{tag}_posc", [128, NT], F32)
    co_sb = sb(ctx, nc, f"{tag}_co", [128, 1], F32)
    ohx = sb(ctx, nc, f"{tag}_ohx", [128, 8], F32)
    osel = sb(ctx, nc, f"{tag}_osel", [128, 1], F32)
    grank = sb(ctx, nc, f"{tag}_grank", [128, NT], F32)
    keep = sb(ctx, nc, f"{tag}_keep", [128, NT], F32)
    slot_f = sb(ctx, nc, f"{tag}_slotf", [128, NT], F32)
    slot_i = sb(ctx, nc, f"{tag}_sloti", [128, NT], I32)
    tokid = sb(ctx, nc, f"{tag}_tokid", [128, NT], F32) if want_tokid else None
    tmp8 = sb(ctx, nc, f"{tag}_tmp8", [128, 8], F32)
    tmp1 = sb(ctx, nc, f"{tag}_tmp1", [128, NT], F32)
    offdram = nc.dram_tensor(f"{tag}_offdram", [4, 8], F32, kind="Internal").ap()
    if poff is None:
        poff = psum(ctx, nc, f"{tag}_poff", (32, 8))

    P.dma("sync", cnt_sb[:], cnt_all, "dl")
    P.dma("sync", mmc_sb[:], mmc, "dl")
    P.dma("sync", msel_sb[:], msel, "dl")
    P.dma("sync", idx_sb[:], idx_pt, "dl")
    P.dma("sync", lr_sb[:], lr_pt, "dl")
    P.dma("sync", posc_sb[:], posc, "dl")
    P.dma("sync", co_sb[:], coreoff, "dl")
    ld = P.cnt["dl"]
    P.wait("tensor", "dl", ld)
    P.op("tensor", lambda En: En.matmul(poff[:, :], mmc_sb[:], cnt_sb[:],
                                        start=True, stop=True), inc="t")
    P.wait("vector", "t")
    P.op("vector", lambda En: En.tensor_copy(off_sb[:], poff[:, :]), inc="v")
    P.wait("tensor", "v")
    P.op("tensor", lambda En: En.matmul(poff[0:4, :], msel_sb[:], off_sb[:],
                                        start=True, stop=True), inc="t")
    P.wait("vector", "t")
    P.op("vector", lambda En: En.tensor_copy(offmy_sb[:], poff[0:4, :]), inc="v")
    P.wait("sync", "v", P.cnt["v"])
    P.dma("sync", offdram, offmy_sb[:], "dx")
    P.wait("sync", "dx", P.cnt["dx"])
    for b in range(4):
        P.dma("sync", offrep[:, b, :], offdram[b:b + 1, :].broadcast_to([128, 8]), "dx")
    reps = P.cnt["dx"]
    P.wait("vector", "dx", reps)
    P.wait("vector", "dl", ld)
    for tt in range(NT):
        b = tt // 4
        P.op("vector", lambda En, tt=tt: En.tensor_sub(
            ohx[:], iota8_sb[:], idx_sb[:, tt:tt + 1].broadcast_to([128, 8])), inc="v")
        P.wait("vector", "v")
        P.op("vector", lambda En: En.tensor_single_scalar(ohx[:], ohx[:], 0.0,
                                                          Alu.is_equal), inc="v")
        P.wait("vector", "v")
        P.op("vector", lambda En, b=b: En.tensor_mul(tmp8[:], ohx[:],
                                                     offrep[:, b, :]), inc="v")
        P.wait("vector", "v")
        P.op("vector", lambda En: En.reduce_sum(osel[:], tmp8[:], axis=AX), inc="v")
        P.wait("vector", "v")
        P.op("vector", lambda En, tt=tt: En.tensor_add(
            grank[:, tt:tt + 1], lr_sb[:, tt:tt + 1], osel[:]), inc="v")
        P.wait("vector", "v")
        P.op("vector", lambda En, tt=tt: En.tensor_single_scalar(
            keep[:, tt:tt + 1], grank[:, tt:tt + 1], float(CAP) - 0.5, Alu.is_lt), inc="v")
        P.op("vector", lambda En, tt=tt: En.tensor_scalar_mul(
            tmp1[:, tt:tt + 1], idx_sb[:, tt:tt + 1], float(CAP)), inc="v")
        P.wait("vector", "v")
        P.op("vector", lambda En, tt=tt: En.tensor_add(
            tmp1[:, tt:tt + 1], tmp1[:, tt:tt + 1], grank[:, tt:tt + 1]), inc="v")
        P.op("vector", lambda En, tt=tt: En.tensor_scalar_add(
            slot_f[:, tt:tt + 1], posc_sb[:, tt:tt + 1], float(E * CAP)), inc="v")
        P.wait("vector", "v")
        P.op("vector", lambda En, tt=tt: En.tensor_sub(
            tmp1[:, tt:tt + 1], tmp1[:, tt:tt + 1], slot_f[:, tt:tt + 1]), inc="v")
        P.wait("vector", "v")
        P.op("vector", lambda En, tt=tt: En.scalar_tensor_tensor(
            grank[:, tt:tt + 1], tmp1[:, tt:tt + 1], keep[:, tt:tt + 1],
            slot_f[:, tt:tt + 1], Alu.mult, Alu.add), inc="v")
        P.wait("vector", "v")
        P.op("vector", lambda En, tt=tt: En.tensor_copy(
            slot_i[:, tt:tt + 1], grank[:, tt:tt + 1]), inc="v")
    if want_tokid:
        P.op("vector", lambda En: En.tensor_scalar_add(tokid[:], posc_sb[:],
                                                       co_sb[:]), inc="v")
    return {"slot_i": slot_i, "keep": keep, "tokid": tokid}



def build_C(debug=False):
    nc = bass.Bass("TRN2", target_bir_lowering=False, debug=False)
    y1all = nc.dram_tensor("y1all", [NCORE * 4 * SL, D], F32, kind="ExternalInput").ap()
    cnt_all = nc.dram_tensor("cnt_all", [32, 8], F32, kind="ExternalInput").ap()
    msel = nc.dram_tensor("msel", [32, 4], F32, kind="ExternalInput").ap()
    idx_pt = nc.dram_tensor("idx_pt", [128, NT], F32, kind="ExternalInput").ap()
    lr_pt = nc.dram_tensor("lr_pt", [128, NT], F32, kind="ExternalInput").ap()
    posc = nc.dram_tensor("posc", [128, NT], F32, kind="ExternalInput").ap()
    coreoff = nc.dram_tensor("coreoff", [128, 1], F32, kind="ExternalInput").ap()
    w1e = nc.dram_tensor("w1e", [D, FF], F32, kind="ExternalInput").ap()
    b1e = nc.dram_tensor("b1e", [128, FF // 128], F32, kind="ExternalInput").ap()
    w2e = nc.dram_tensor("w2e", [FF, D], F32, kind="ExternalInput").ap()
    b2e = nc.dram_tensor("b2e", [1, D], F32, kind="ExternalInput").ap()
    iota8c = nc.inline_tensor(np.tile(np.arange(8, dtype=np.float32), (128, 1)), "c_iota8").ap()
    onesrow = nc.inline_tensor(np.ones((1, 128), BF), "c_onesrow").ap()

    y_out = nc.dram_tensor("Y", [CAP, D], F32, kind="ExternalOutput").ap()
    if debug:
        dbg_inv = nc.dram_tensor("dbg_inv", [128, NT], F32, kind="ExternalOutput").ap()
        dbg_slot = nc.dram_tensor("dbg_slot", [128, NT], F32, kind="ExternalOutput").ap()
        dbg_xtm = nc.dram_tensor("dbg_xtm", [128, 8, NT, 128], F32, kind="ExternalOutput").ap()
        dbg_h = nc.dram_tensor("dbg_h", [4, 128, 32, 512], F32, kind="ExternalOutput").ap()
    inv_dram = nc.dram_tensor("inv_dram", [E * CAP + 2 * CAP, 1], F32, kind="Internal").ap()
    inv_shard = nc.dram_tensor("inv_shard", [1, CAP], F32, kind="Internal").ap()

    with ExitStack() as ctx:
        P = Prog(nc, ctx)
        iota8_sb = sb(ctx, nc, "iota8_sb", [128, 8], F32)
        onesrow_sb = sb(ctx, nc, "onesrow_sb", [1, 128], BF16)
        zeros_sb = sb(ctx, nc, "zeros_sb", [128, 160], F32)
        invf = sb(ctx, nc, "invf", [128, NT], F32)
        invi = sb(ctx, nc, "invi", [128, NT], I32)
        xrow = [sb(ctx, nc, f"xrow{i}", [128, D], F32) for i in range(2)]
        X_tm = sb(ctx, nc, "X_tm", [128, 8, NT, 128], BF16)
        X_fm = sb(ctx, nc, "X_fm", [128, 8, 4 * SL], BF16)
        h_fm = sb(ctx, nc, "h_fm", [128, 32, 512], BF16)
        w1sb = [sb(ctx, nc, f"w1sb{i}", [128, 8, 128], BF16) for i in range(2)]
        w2sb = [sb(ctx, nc, f"w2sb{i}", [128, D], BF16) for i in range(2)]
        b1_sb = sb(ctx, nc, "b1_sb", [128, FF // 128], F32)
        b2_sb = sb(ctx, nc, "b2_sb", [1, D], BF16)
        yout_sb = sb(ctx, nc, "yout_sb", [128, D], F32)
        pb = [psum(ctx, nc, f"pb{i}") for i in range(8)]

        P.dma("sync", iota8_sb[:], iota8c, "dl")
        P.dma("sync", b1_sb[:], b1e, "dl")
        P.dma("gpsimd", b2_sb[:], b2e, "dg")
        P.dma("gpsimd", onesrow_sb[:], onesrow, "dg")

        st = emit_slots(P, nc, ctx, "cs", cnt_all, msel, idx_pt, lr_pt, posc,
                        coreoff, iota8_sb, want_tokid=True, poff=pb[7][0:32, 0:8])
        slots_done = P.cnt["v"]

        P.op("vector", lambda En: En.memset(zeros_sb[:], 0.0), inc="v")
        P.wait("sync", "v", P.cnt["v"])
        for blk in range(1):
            P.dma("sync", inv_dram[:, 0].rearrange("(p a) -> p a", p=128),
                  zeros_sb[:, 0:(E * CAP + 2 * CAP) // 128], "dx")
        P.wait("gpsimd", "dx", P.cnt["dx"])
        P.wait("gpsimd", "v", slots_done)
        for t in range(NT):
            P.op("gpsimd", lambda En, t=t: En.indirect_dma_start(
                inv_dram[:, :], IndirectOffsetOnAxis(st["slot_i"][:, t:t + 1], 0),
                st["tokid"][:, t:t + 1], None), inc="dgi", k=16)
        P.wait("gpsimd", "dgi", P.cnt["dgi"])
        P.op("gpsimd", lambda En: En.collective_compute(
            "ReduceScatter", Alu.add,
            ins=[inv_dram[0:E * CAP, 0].rearrange("(e r) -> e r", e=NCORE)],
            outs=[inv_shard[:]],
            replica_groups=[list(range(NCORE))]), inc="cc", k=1)
        P.wait("sync", "cc", P.cnt["cc"])
        P.dma("sync", invf[:], inv_shard[0, :].rearrange("(p t) -> p t", p=128), "dl")
        P.wait("vector", "dl", P.cnt["dl"])
        P.op("vector", lambda En: En.tensor_copy(invi[:], invf[:]), inc="v")
        inv_done = P.wait("gpsimd", "v", P.cnt["v"])
        for t in range(NT):
            P.op("gpsimd", lambda En, t=t: En.indirect_dma_start(
                xrow[t % 2][:, :], None,
                y1all[:, :], IndirectOffsetOnAxis(invi[:, t:t + 1], 0)), inc="dgi", k=16)
            gv = P.cnt["dgi"]
            P.wait("vector", "dgi", gv)
            P.op("vector", lambda En, t=t: En.tensor_copy(
                X_tm[:, :, t, :], xrow[t % 2][:].rearrange("p (k c) -> p k c", k=8)), inc="v")
            if t + 1 < NT:
                P.wait("gpsimd", "v", P.cnt["v"])
        if debug:
            P.wait("sync", "v", P.cnt["v"])
            P.dma("sync", dbg_inv, invf[:], "do")
            P.op("vector", lambda En: En.tensor_copy(invf[:], st["slot_i"][:]), inc="v")
            P.wait("sync", "v", P.cnt["v"])
            P.dma("sync", dbg_slot, invf[:], "do")
            P.wait("gpsimd", "v", P.cnt["v"])
            P.dma("gpsimd", dbg_xtm, X_tm[:], "dg")
        P.wait("scalar", "v", P.cnt["v"])
        for k in range(8):
            P.tdma(X_fm[:, k, :].rearrange("p (t c) -> p t c", t=NT), X_tm[:, k, :, :])
        xfm_done = P.cnt["ds"]
        P.wait("tensor", "ds", xfm_done)
        P.wait("tensor", "dl", None)
        P.wait("tensor", "dg", None)
        P.wait("gpsimd", "ds", xfm_done)

        P.wait("vector", "dl", None)
        for ts in range(4):
            t_after = {}
            for ffb in range(32):
                buf = w1sb[ffb % 2]
                if ffb >= 2:
                    P.wait("gpsimd", "t", t_after[ffb - 2])
                dv = P.dma("gpsimd", buf[:],
                           w1e[:, ffb * 128:(ffb + 1) * 128].rearrange("(k p) f -> p k f", p=128),
                           "dg")
                P.wait("tensor", "dg", dv)
                pbank = pb[ffb % 2]
                for k in range(8):
                    P.op("tensor", lambda En, k=k, ts=ts, buf=buf, pbank=pbank: En.matmul(
                        pbank[:, :], buf[:, k, :],
                        X_fm[:, k, ts * 512:(ts + 1) * 512],
                        start=(k == 0), stop=(k == 7)), inc=("t" if k == 7 else None))
                t_after[ffb] = P.cnt["t"]
                P.wait("vector", "t", t_after[ffb])
                P.op("vector", lambda En, ffb=ffb, pbank=pbank: En.tensor_scalar_add(
                    h_fm[:, ffb, :], pbank[:, :], b1_sb[:, ffb:ffb + 1]), inc="v")
                P.wait("tensor", "v")
            if debug:
                P.wait("gpsimd", "v", P.cnt["v"])
                P.dma("gpsimd", dbg_h[ts], h_fm[:], "dg")
            hv = P.cnt["v"]
            P.wait("tensor", "v", hv)
            for ffc in range(32):
                buf = w2sb[ffc % 2]
                dv = P.dma("gpsimd", buf[:], w2e[ffc * 128:(ffc + 1) * 128, :], "dg")
                P.wait("tensor", "dg", dv)
                for sub in range(4):
                    for ob in range(2):
                        P.op("tensor", lambda En, ffc=ffc, sub=sub, ob=ob, buf=buf, ts=ts:
                             En.matmul(pb[sub * 2 + ob][:, :],
                                       h_fm[:, ffc, sub * 128:(sub + 1) * 128],
                                       buf[:, ob * 512:(ob + 1) * 512],
                                       start=(ffc == 0), stop=False),
                             inc=None)
                P.op("tensor", lambda En: En.nop(), inc="t")
                if ffc < 31:
                    P.wait("gpsimd", "t", P.cnt["t"])
            for sub in range(4):
                for ob in range(2):
                    P.op("tensor", lambda En, sub=sub, ob=ob: En.matmul(
                        pb[sub * 2 + ob][:, :], onesrow_sb[0:1, :],
                        b2_sb[0:1, ob * 512:(ob + 1) * 512],
                        start=False, stop=True), inc="t")
            tdone = P.cnt["t"]
            P.wait("scalar", "t", tdone)
            for sub in range(4):
                r0 = ts * 4 + sub
                for ob in range(2):
                    P.op("scalar", lambda En, sub=sub, ob=ob: En.copy(
                        yout_sb[:, ob * 512:(ob + 1) * 512], pb[sub * 2 + ob][:, :]), inc="s")
                P.wait("sync", "s", P.cnt["s"])
                P.dma("sync", y_out.rearrange("(p t) d -> p t d", t=NT)[:, r0, :],
                      yout_sb[:], "do")
                P.wait("scalar", "do", P.cnt["do"])
            P.wait("tensor", "s", P.cnt["s"])
        P.wait("sync", "do", P.cnt["do"])
        P.finish()
    return nc



def build_D():
    nc = bass.Bass("TRN2", target_bir_lowering=False, debug=False)
    xs = nc.dram_tensor("xs", [4 * SL, D], F32, kind="ExternalInput").ap()
    y1s = nc.dram_tensor("y1s", [4 * SL, D], F32, kind="ExternalInput").ap()
    yall = nc.dram_tensor("yall", [E * CAP, D], F32, kind="ExternalInput").ap()
    o_fm = nc.dram_tensor("o_fm", [D, 4 * SL], F32, kind="ExternalInput").ap()
    wo = nc.dram_tensor("wo", [D, D], F32, kind="ExternalInput").ap()
    rmsw = nc.dram_tensor("rmsw", [128, 8], F32, kind="ExternalInput").ap()
    ln2g = nc.dram_tensor("ln2g", [1, D], F32, kind="ExternalInput").ap()
    ln2b = nc.dram_tensor("ln2b", [1, D], F32, kind="ExternalInput").ap()
    gv_pt = nc.dram_tensor("gv_pt", [128, NT], F32, kind="ExternalInput").ap()
    cnt_all = nc.dram_tensor("cnt_all", [32, 8], F32, kind="ExternalInput").ap()
    msel = nc.dram_tensor("msel", [32, 4], F32, kind="ExternalInput").ap()
    idx_pt = nc.dram_tensor("idx_pt", [128, NT], F32, kind="ExternalInput").ap()
    lr_pt = nc.dram_tensor("lr_pt", [128, NT], F32, kind="ExternalInput").ap()
    posc = nc.dram_tensor("posc", [128, NT], F32, kind="ExternalInput").ap()
    coreoff = nc.dram_tensor("coreoff", [128, 1], F32, kind="ExternalInput").ap()
    iota8c = nc.inline_tensor(np.tile(np.arange(8, dtype=np.float32), (128, 1)), "c_iota8").ap()

    out = nc.dram_tensor("out", [4 * SL, D], F32, kind="ExternalOutput").ap()

    with ExitStack() as ctx:
        P = Prog(nc, ctx)
        iota8_sb = sb(ctx, nc, "iota8_sb", [128, 8], F32)
        S = LNS(ctx, nc, "lnD")
        g_rep = sb(ctx, nc, "g_rep", [128, D], F32)
        b_rep = sb(ctx, nc, "b_rep", [128, D], F32)
        gv_sb = sb(ctx, nc, "gv_sb", [128, NT], F32)
        gk = sb(ctx, nc, "gk", [128, NT], F32)
        slotc = sb(ctx, nc, "slotc", [128, NT], I32)
        yrow = [sb(ctx, nc, f"yrow{i}", [128, D], F32) for i in range(2)]
        xt = sb(ctx, nc, "xt", [128, D], F32)
        y1t = sb(ctx, nc, "y1t", [128, D], F32)
        acc = sb(ctx, nc, "acc", [128, D], F32)
        y2t = sb(ctx, nc, "y2t", [128, D], F32)
        ofm_sb = sb(ctx, nc, "ofm_sb", [128, 8, 4 * SL], BF16)
        wo_sb = sb(ctx, nc, "wo_sb", [128, 8, D], BF16)
        rms_sb = sb(ctx, nc, "rms_sb", [128, 8], F32)
        pb = [psum(ctx, nc, f"pb{i}") for i in range(8)]

        P.dma("sync", iota8_sb[:], iota8c, "dl")
        P.dma("sync", g_rep[:], ln2g[0:1, :].broadcast_to([128, D]), "dl")
        P.dma("sync", b_rep[:], ln2b[0:1, :].broadcast_to([128, D]), "dl")
        P.dma("sync", gv_sb[:], gv_pt, "dl")
        P.dma("sync", rms_sb[:], rmsw, "dl")
        ld0 = P.cnt["dl"]
        P.dma("gpsimd", wo_sb[:], wo.rearrange("(k p) o -> p k o", p=128), "dg")
        P.wait("vector", "dg", P.cnt["dg"])
        P.wait("vector", "dl", ld0)
        for k in range(8):
            P.op("vector", lambda En, k=k: En.tensor_scalar_mul(
                wo_sb[:, k, :], wo_sb[:, k, :], rms_sb[:, k:k + 1]), inc="v")
        P.dma("gpsimd", ofm_sb[:], o_fm.rearrange("(k p) t -> p k t", p=128), "dg")
        ofm_done = P.cnt["dg"]

        st = emit_slots(P, nc, ctx, "ds", cnt_all, msel, idx_pt, lr_pt, posc,
                        coreoff, iota8_sb, want_tokid=False, poff=pb[7][0:32, 0:8])
        P.wait("vector", "v", P.cnt["v"])
        P.op("vector", lambda En: En.tensor_copy(gk[:], st["slot_i"][:]), inc="v")
        P.wait("vector", "v")
        P.op("vector", lambda En: En.tensor_mul(gk[:], gk[:], st["keep"][:]), inc="v")
        P.wait("vector", "v")
        P.op("vector", lambda En: En.tensor_copy(slotc[:], gk[:]), inc="v")
        P.wait("vector", "v")
        P.op("vector", lambda En: En.tensor_mul(gk[:], gv_sb[:], st["keep"][:]), inc="v")
        slots_done = P.cnt["v"]
        P.wait("gpsimd", "v", slots_done)

        for tt in range(NT):
            P.op("gpsimd", lambda En, tt=tt: En.indirect_dma_start(
                yrow[tt % 2][:, :], None,
                yall[:, :], IndirectOffsetOnAxis(slotc[:, tt:tt + 1], 0)), inc="dgi", k=16)
            gdone = P.cnt["dgi"]
            d1 = P.dma("sync", xt[:], xs[tt * 128:(tt + 1) * 128, :], "dl")
            d2 = P.dma("sync", y1t[:], y1s[tt * 128:(tt + 1) * 128, :], "dl")
            P.wait("vector", "dl", d2)
            P.wait("vector", "dgi", gdone)
            P.op("vector", lambda En, tt=tt: En.scalar_tensor_tensor(
                acc[:], yrow[tt % 2][:], gk[:, tt:tt + 1], y1t[:],
                Alu.mult, Alu.add), inc="v")
            if tt + 1 < NT:
                P.wait("gpsimd", "v", P.cnt["v"])
            P.wait("scalar", "v", P.cnt["v"])
            ln_tile(P, acc[:], y2t[:], S, gb=(g_rep, b_rep))
            P.op("vector", lambda En: En.tensor_add(y2t[:], y2t[:], xt[:]), inc="v")
            fin_v = P.cnt["v"]
            if tt == 0:
                P.wait("tensor", "dg", ofm_done)
            for k in range(8):
                for ob in range(2):
                    P.op("tensor", lambda En, k=k, ob=ob, tt=tt: En.matmul(
                        pb[ob][:, :],
                        ofm_sb[:, k, tt * 128:(tt + 1) * 128],
                        wo_sb[:, k, ob * 512:(ob + 1) * 512],
                        start=(k == 0), stop=(k == 7)),
                        inc=("t" if k == 7 and ob == 1 else None))
            tdone = P.cnt["t"]
            P.wait("vector", "t", tdone)
            for ob in range(2):
                P.op("vector", lambda En, ob=ob: En.tensor_add(
                    y2t[:, ob * 512:(ob + 1) * 512], y2t[:, ob * 512:(ob + 1) * 512],
                    pb[ob][:, :]), inc="v")
            vdone = P.cnt["v"]
            P.wait("sync", "v", vdone)
            P.dma("sync", out[tt * 128:(tt + 1) * 128, :], y2t[:], "do")
            P.wait("tensor", "v", vdone)
            P.wait("vector", "do", P.cnt["do"])
        P.wait("sync", "do", P.cnt["do"])
        P.finish()
    return nc





AW = 512
NW = L // AW
NSW = AW // 128


def _mask_bd_neg():
    m = np.zeros((128, 128), np.float32)
    for b_ in range(4):
        s = b_ * 32
        m[s:s + 32, s:s + 32] = -np.tril(np.ones((32, 32)), -1)
    return m


def _mask_blk_neg():
    m = np.zeros((128, 128), np.float32)
    for bi in range(4):
        m[bi * 32:(bi + 1) * 32, :bi * 32] = -1.0
    return m


def build_A(debug=False, dbg_w=NW - 1):
    nc = bass.Bass("TRN2", target_bir_lowering=False, debug=False)
    xb = nc.dram_tensor("xb", [L, D], F32, kind="ExternalInput").ap()
    wq_s = nc.dram_tensor("wq_s", [D, 512], F32, kind="ExternalInput").ap()
    wk_s = nc.dram_tensor("wk_s", [D, 512], F32, kind="ExternalInput").ap()
    wv_s = nc.dram_tensor("wv_s", [D, 512], F32, kind="ExternalInput").ap()
    wb_s = nc.dram_tensor("wb_s", [D, 2], F32, kind="ExternalInput").ap()
    cq_s = nc.dram_tensor("cq_s", [128, 4, 4], F32, kind="ExternalInput").ap()
    ck_s = nc.dram_tensor("ck_s", [128, 4, 4], F32, kind="ExternalInput").ap()
    cv_s = nc.dram_tensor("cv_s", [128, 4, 4], F32, kind="ExternalInput").ap()
    w7_s = nc.dram_tensor("w7_s", [128, 4, 7], F32, kind="ExternalInput").ap()
    wf1 = nc.dram_tensor("wf1", [D, 2 * D], F32, kind="ExternalInput").ap()
    bf1_pt = nc.dram_tensor("bf1_pt", [128, 16], F32, kind="ExternalInput").ap()
    wf2_s = nc.dram_tensor("wf2_s", [2 * D, 8], F32, kind="ExternalInput").ap()
    bf2_s = nc.dram_tensor("bf2_s", [1, 8], F32, kind="ExternalInput").ap()
    tlp = nc.dram_tensor("tlp", [512, 64, 128], F32, kind="ExternalInput").ap()
    tlc = nc.dram_tensor("tlc", [512, 128, 128], F32, kind="ExternalInput").ap()

    o_out = nc.dram_tensor("o_slice", [L, 512], F32, kind="ExternalOutput").ap()
    dbg = {}
    if debug:
        for nm, shp in [("dbg_QS", [128, 256]), ("dbg_sbf", [128, 256]),
                        ("dbg_p1", [128, 256]), ("dbg_p2", [128, 256]),
                        ("dbg_p3", [128, 256]),
                        ("dbg_qcopy", [128, 2, 128]),
                        ("dbg_S", [128, 2, 2, 256]), ("dbg_RHS", [128, 256]),
                        ("dbg_Ut1", [128, 256]), ("dbg_N1", [128, 128]),
                        ("dbg_T11", [128, 256]),
                        ("dbg_X", [128, 128]), ("dbg_Y", [128, 128]),
                        ("dbg_Pm", [128, 128]), ("dbg_U", [128, 256]),
                        ("dbg_Wt", [128, 2, 128]), ("dbg_Nn", [128, 128]),
                        ("dbg_Ut", [128, 256]), ("dbg_Ct", [128, 128]),
                        ("dbg_beta", [128, NSW, 2]),
                        ("dbg_q", [128, 4, AW]), ("dbg_k", [128, 4, AW]),
                        ("dbg_v", [128, 4, AW + 6]), ("dbg_delta", [128, NSW, 512]),
                        ("dbg_fw", [128, 32, 8]), ("dbg_short", [128, NSW, 4, 128]),
                        ("dbg_acc", [128, 32, 512])]:
            dbg[nm] = nc.dram_tensor(nm, shp, F32, kind="ExternalOutput").ap()

    wf1bf = nc.dram_tensor("wf1bf", [D, 2 * D], BF16, kind="Internal").ap()
    rsq_dram = nc.dram_tensor("rsq_dram", [1, 4 * AW], BF16, kind="Internal").ap()
    beta_dram = nc.dram_tensor("beta_dram", [2, AW], BF16, kind="Internal").ap()

    m_bd = nc.inline_tensor(_mask_bd_neg().astype(BF), "c_mbd").ap()
    m_bdT = nc.inline_tensor(_mask_bd_neg().T.copy().astype(BF), "c_mbdT").ap()
    m_blk = nc.inline_tensor(_mask_blk_neg().astype(BF), "c_mblk").ap()
    m_blkT = nc.inline_tensor(_mask_blk_neg().T.copy().astype(BF), "c_mblkT").ap()
    m_triu = nc.inline_tensor(np.triu(np.ones((128, 128), np.float32)).astype(BF), "c_mtriu").ap()
    ident = nc.inline_tensor(np.eye(128, dtype=np.float32).astype(BF), "c_ident").ap()
    onesrow_c = nc.inline_tensor(np.ones((1, 128), BF), "c_onesrow").ap()
    onescol_c = nc.inline_tensor(np.ones((128, 1), BF), "c_onescol").ap()

    with ExitStack() as ctx:
        P = Prog(nc, ctx)
        v_tm = sb(ctx, nc, "v_tm", [128, 32, 512], BF16)
        acc_tm = sb(ctx, nc, "acc_tm", [128, 32, 512], BF16)
        fw_tm = sb(ctx, nc, "fw_tm", [128, 32, 8], F32)
        wq_sb = sb(ctx, nc, "wq_sb", [128, 8, 512], BF16)
        wk_sb = sb(ctx, nc, "wk_sb", [128, 8, 512], BF16)
        wv_sb = sb(ctx, nc, "wv_sb", [128, 8, 512], BF16)
        wb_sb = sb(ctx, nc, "wb_sb", [128, 8, 2], BF16)
        cw_sb = sb(ctx, nc, "cw_sb", [128, 3, 4, 4], F32)
        w7_sb = sb(ctx, nc, "w7_sb", [128, 4, 7], F32)
        wf2_sb = sb(ctx, nc, "wf2_sb", [128, 16, 8], BF16)
        bf1_sb = sb(ctx, nc, "bf1_sb", [128, 16], F32)
        bf2_sb = sb(ctx, nc, "bf2_sb", [1, 8], BF16)
        mbd_sb = sb(ctx, nc, "mbd_sb", [128, 128], BF16)
        mbdT_sb = sb(ctx, nc, "mbdT_sb", [128, 128], BF16)
        mblk_sb = sb(ctx, nc, "mblk_sb", [128, 128], BF16)
        mblkT_sb = sb(ctx, nc, "mblkT_sb", [128, 128], BF16)
        mtriu_sb = sb(ctx, nc, "mtriu_sb", [128, 128], BF16)
        id_sb = sb(ctx, nc, "id_sb", [128, 128], BF16)
        onesrow_sb = sb(ctx, nc, "onesrow_sb", [1, 128], BF16)
        onescol_sb = sb(ctx, nc, "onescol_sb", [128, 1], BF16)
        xt = sb(ctx, nc, "xt", [128, D], F32)
        xnf = sb(ctx, nc, "xnf", [128, D], F32)
        S = LNS(ctx, nc, "lnA")
        uA = sb(ctx, nc, "uA", [128, 8192], BF16)
        xn_tm = uA[:, 0:4096].rearrange("p (k t c) -> p k t c", k=8, t=NSW)
        xn_fm = uA[:, 4096:8192].rearrange("p (k c) -> p k c", k=8)
        rawu = sb(ctx, nc, "rawu", [128, 4 * (AW + 3)], BF16)
        raw = rawu[:].rearrange("p (a c) -> p a c", a=4)
        qh_ = sb(ctx, nc, "qh_", [128, 4, 3], BF16)
        kh_ = sb(ctx, nc, "kh_", [128, 4, 3], BF16)
        vh_ = sb(ctx, nc, "vh_", [128, 4, 6], BF16)
        vrh_ = sb(ctx, nc, "vrh_", [128, 4, 3], BF16)
        q_fm = sb(ctx, nc, "q_fm", [128, 4, AW], BF16)
        k_fm = sb(ctx, nc, "k_fm", [128, 4, AW], BF16)
        kb_fm = sb(ctx, nc, "kb_fm", [128, 4, AW], BF16)
        v_fm = sb(ctx, nc, "v_fm", [128, 4, AW + 6], BF16)
        cA = sb(ctx, nc, "cA", [128, AW], F32)
        cB = sb(ctx, nc, "cB", [128, AW], F32)
        cC = sb(ctx, nc, "cC", [128, AW], F32)
        cD = sb(ctx, nc, "cD", [128, AW], F32)
        k_tm = sb(ctx, nc, "k_tm", [128, NSW, 4, 128], BF16)
        kb_tm = sb(ctx, nc, "kb_tm", [128, 2, 128], BF16)
        vb_tm = sb(ctx, nc, "vb_tm", [128, 256], BF16)
        beta_bf = sb(ctx, nc, "beta_bf", [16, AW], BF16)
        beta_tm = sb(ctx, nc, "beta_tm", [128, NSW, 16], BF16)
        beta_tmf = sb(ctx, nc, "beta_tmf", [128, NSW, 2], F32)
        brep = sb(ctx, nc, "brep", [128, AW], BF16)
        rrep = sb(ctx, nc, "rrep", [128, AW], BF16)
        rsq_f = sb(ctx, nc, "rsq_f", [1, 4 * AW], F32)
        rsq_bf = sb(ctx, nc, "rsq_bf", [1, 4 * AW], BF16)
        rsq_f2 = sb(ctx, nc, "rsq_f2", [1, 4 * AW], F32)
        dt = {}
        for nm in ["X", "Y", "X2", "Y2", "X4", "Y4", "X8", "Y8", "X16", "Y16",
                   "Pm", "Ct", "Nn", "Nt", "N2", "N2t"]:
            dt[nm] = sb(ctx, nc, f"dt_{nm}", [128, 128], BF16)
        dU = sb(ctx, nc, "dU", [128, 256], BF16)
        dRHS = sb(ctx, nc, "dRHS", [128, 256], BF16)
        dT1 = sb(ctx, nc, "dT1", [128, 256], BF16)
        dUt = sb(ctx, nc, "dUt", [128, 256], BF16)
        wt_sb = sb(ctx, nc, "wt_sb", [128, 2, 128], BF16)
        S_f = sb(ctx, nc, "S_f", [128, 2, 2, 256], F32)
        S_bf = sb(ctx, nc, "S_bf", [128, 2, 2, 256], BF16)
        delta_tm = sb(ctx, nc, "delta_tm", [128, NSW, 512], BF16)
        wf1blk = [sb(ctx, nc, f"wf1blk{i}", [128, 8, 128], BF16) for i in range(2)]
        h1blk = sb(ctx, nc, "h1blk", [128, AW], BF16)
        lgt_s = sb(ctx, nc, "lgt_s", [128, 8], F32)
        fmax = sb(ctx, nc, "fmax", [128, 2], F32)
        fsum = sb(ctx, nc, "fsum", [128, 2], F32)
        fexp = sb(ctx, nc, "fexp", [128, 8], F32)
        short_fm = sb(ctx, nc, "short_fm", [128, 4, AW], BF16)
        short_tm = rawu[:, 0:2048].rearrange("p (t a c) -> p t a c", t=NSW, a=4)
        mtU = sb(ctx, nc, "mtU", [128, 512], F32)
        mt0 = mtU[:, 0:256]
        mt1 = mtU[:, 256:512]
        cE = mtU[:, 0:512]
        cCb = cC[:].bitcast(BF16)
        cDb = cD[:].bitcast(BF16)
        tlc_sb = [cCb[:, 0:128], cCb[:, 128:256]]
        tlp_sb = [cCb[:, 256:384], cCb[:, 384:512]]
        long_h = uA[:, 0:8192].rearrange("p (t c) -> p t c", t=32)
        of32 = cD[:, 0:256]
        sqd = cA[:, 0:256]
        rmss = cB[:, 0:1]
        rmsr = cB[:, 1:2]

        pb = [psum(ctx, nc, f"pb{i}") for i in range(8)]

        for ap_, t_ in [(m_bd, mbd_sb), (m_bdT, mbdT_sb), (m_blk, mblk_sb),
                        (m_blkT, mblkT_sb), (m_triu, mtriu_sb), (ident, id_sb),
                        (onesrow_c, onesrow_sb), (onescol_c, onescol_sb)]:
            P.dma("sync", t_[:], ap_, "dl")
        P.dma("sync", cw_sb[:, 0, :, :], cq_s, "dl")
        P.dma("sync", cw_sb[:, 1, :, :], ck_s, "dl")
        P.dma("sync", cw_sb[:, 2, :, :], cv_s, "dl")
        P.dma("sync", w7_sb[:], w7_s, "dl")
        P.dma("sync", bf1_sb[:], bf1_pt, "dl")
        setup_dl = P.cnt["dl"]
        P.dma("gpsimd", wq_sb[:], wq_s.rearrange("(k p) o -> p k o", p=128), "dg")
        P.dma("gpsimd", wk_sb[:], wk_s.rearrange("(k p) o -> p k o", p=128), "dg")
        P.dma("gpsimd", wv_sb[:], wv_s.rearrange("(k p) o -> p k o", p=128), "dg")
        P.dma("gpsimd", wb_sb[:], wb_s.rearrange("(k p) o -> p k o", p=128), "dg")
        P.dma("gpsimd", wf2_sb[:], wf2_s.rearrange("(k p) o -> p k o", p=128), "dg")
        P.dma("gpsimd", bf2_sb[:], bf2_s, "dg")
        P.dma("gpsimd", wf1bf, wf1, "dg")
        setup_dg = P.cnt["dg"]
        P.wait("scalar", "dg", setup_dg)
        P.wait("scalar", "dl", setup_dl)
        P.wait("tensor", "dg", setup_dg)
        P.wait("tensor", "dl", setup_dl)
        P.wait("vector", "dl", setup_dl)
        P.op("vector", lambda En: En.memset(S_f[:], 0.0), inc="v")
        P.op("vector", lambda En: En.memset(S_bf[:], 0.0), inc="v")

        def evac(dst_ap, bank_ap, mask=None, sem="v", eng="vector"):
            P.wait(eng, "t")
            if mask is None:
                P.op(eng, lambda En: En.tensor_copy(dst_ap, bank_ap), inc=sem)
            else:
                P.op(eng, lambda En: En.tensor_mul(dst_ap, bank_ap, mask[:]), inc=sem)
            P.wait("tensor", sem)

        for w in range(NW):
            for lt in range(NSW):
                tok0 = w * AW + lt * 128
                dv = P.dma("sync", xt[:], xb[tok0:tok0 + 128, :], "dl")
                P.wait("vector", "dl", dv)
                P.wait("scalar", "dl", dv)
                ln_tile(P, xt[:], xnf[:], S, out_bf=xn_tm[:, :, lt, :])
                P.wait("sync", "s", P.cnt["s"])
            P.wait("scalar", "v", P.cnt["v"])
            for k in range(8):
                P.tdma(xn_fm[:, k, :].rearrange("p (t c) -> p t c", t=NSW),
                       xn_tm[:, k, :, :])
            P.wait("tensor", "ds", P.cnt["ds"])

            for pi, (wsb, fin, halo, foff) in enumerate([
                    (wq_sb, q_fm, qh_, 0), (wk_sb, k_fm, kh_, 0),
                    (wv_sb, v_fm, vrh_, 6)]):
                for kk in range(4):
                    for k in range(8):
                        P.op("tensor", lambda En, k=k, kk=kk, wsb=wsb: En.matmul(
                            pb[kk % 2][:, :], wsb[:, k, kk * 128:(kk + 1) * 128],
                            xn_fm[:, k, :],
                            start=(k == 0), stop=(k == 7)),
                            inc=("t" if k == 7 else None))
                    tdone = P.cnt["t"]
                    P.wait("scalar", "t", tdone)
                    P.op("scalar", lambda En, kk=kk: En.copy(
                        raw[:, kk, 3:3 + AW], pb[kk % 2][:, :]), inc="s")
                    P.wait("tensor", "s")
                P.wait("vector", "s", P.cnt["s"])
                if w == 0:
                    P.op("vector", lambda En: En.memset(raw[:, :, 0:3], 0.0), inc="v")
                else:
                    P.op("vector", lambda En, halo=halo: En.tensor_copy(
                        raw[:, :, 0:3], halo[:]), inc="v")
                P.op("vector", lambda En, halo=halo: En.tensor_copy(
                    halo[:], raw[:, :, AW:AW + 3]), inc="v")
                P.wait("gpsimd", "v", P.cnt["v"])
                for kk in range(4):
                    eng = "vector" if kk % 2 == 0 else "gpsimd"
                    sem = "v" if kk % 2 == 0 else "g"
                    s0, s1 = (cA, cB) if kk % 2 == 0 else (cC, cD)
                    P.op(eng, lambda En, kk=kk, pi=pi, s0=s0: En.tensor_scalar_mul(
                        s0[:, :], raw[:, kk, 0:AW], cw_sb[:, pi, kk, 0:1]), inc=sem)
                    P.wait(eng, sem)
                    bufs = [s0, s1, s0, s1]
                    for tap in range(1, 4):
                        src_b, dst_b = bufs[tap - 1], bufs[tap]
                        if eng == "vector":
                            P.op(eng, lambda En, kk=kk, pi=pi, tap=tap, src_b=src_b, dst_b=dst_b:
                                 En.scalar_tensor_tensor(
                                     dst_b[:, :], raw[:, kk, tap:tap + AW],
                                     cw_sb[:, pi, kk, tap:tap + 1], src_b[:, :],
                                     Alu.mult, Alu.add), inc=sem)
                            P.wait(eng, sem)
                        else:
                            P.op(eng, lambda En, kk=kk, pi=pi, tap=tap: En.tensor_scalar_mul(
                                cE[:, :], raw[:, kk, tap:tap + AW],
                                cw_sb[:, pi, kk, tap:tap + 1]), inc=sem)
                            P.wait(eng, sem)
                            P.op(eng, lambda En, src_b=src_b, dst_b=dst_b: En.tensor_add(
                                dst_b[:, :], src_b[:, :], cE[:, :]), inc=sem)
                            P.wait(eng, sem)
                    P.wait("scalar", sem)
                    P.op("scalar", lambda En, kk=kk, fin=fin, foff=foff, dst_b=bufs[3]:
                         En.activation(fin[:, kk, foff:foff + AW], dst_b[:, :],
                                       Act.Silu), inc="s")
                    P.wait(eng, "s")
            sil_done = P.cnt["s"]
            P.wait("vector", "s", sil_done)
            if w == 0:
                P.op("vector", lambda En: En.memset(v_fm[:, :, 0:6], 0.0), inc="v")
            else:
                P.op("vector", lambda En: En.tensor_copy(v_fm[:, :, 0:6], vh_[:]), inc="v")
            P.op("vector", lambda En: En.tensor_copy(vh_[:], v_fm[:, :, AW:AW + 6]), inc="v")

            P.wait("tensor", "s", sil_done)
            for k in range(8):
                P.op("tensor", lambda En, k=k: En.matmul(
                    pb[2][0:2, :], wb_sb[:, k, :], xn_fm[:, k, :],
                    start=(k == 0), stop=(k == 7)), inc=("t" if k == 7 else None))
            P.wait("scalar", "t")
            P.op("scalar", lambda En: En.activation(beta_bf[0:2, :], pb[2][0:2, :],
                                                    Act.Sigmoid), inc="s")
            P.wait("sync", "s", P.cnt["s"])
            P.dma("sync", beta_dram, beta_bf[0:2, :], "dx")
            P.wait("sync", "dx", P.cnt["dx"])
            P.wait("tensor", "s")

            for ti, fin in enumerate([q_fm, k_fm]):
                for hh in range(2):
                    row = ti * 2 + hh
                    for ci in range(2):
                        kk = hh * 2 + ci
                        P.wait("scalar", "t")
                        P.op("scalar", lambda En, fin=fin, kk=kk: En.activation(
                            h1blk[:, :], fin[:, kk, 0:AW], Act.Square), inc="s")
                        P.wait("tensor", "s")
                        P.op("tensor", lambda En, row=row, ci=ci: En.matmul(
                            pb[3 + row][0:1, :], onescol_sb[:, :], h1blk[:, :],
                            start=(ci == 0), stop=(ci == 1)), inc="t")
            P.wait("vector", "t")
            for row in range(4):
                P.op("vector", lambda En, row=row: En.tensor_copy(
                    rsq_f[0:1, row * AW:(row + 1) * AW], pb[3 + row][0:1, :]), inc="v")
            P.wait("scalar", "v")
            P.wait("tensor", "v")
            P.op("scalar", lambda En: En.activation(rsq_f[:], rsq_f[:],
                                                    Act.Sqrt), inc="s")
            P.wait("vector", "s")
            P.op("vector", lambda En: En.reciprocal(rsq_f2[:], rsq_f[:]), inc="v")
            P.wait("vector", "v")
            P.op("vector", lambda En: En.tensor_copy(rsq_bf[:], rsq_f2[:]), inc="v")
            P.wait("sync", "v", P.cnt["v"])
            P.dma("sync", rsq_dram, rsq_bf[:], "dx")
            P.wait("sync", "dx", P.cnt["dx"])
            for ti, fin in enumerate([q_fm, k_fm]):
                for hh in range(2):
                    row = ti * 2 + hh
                    P.dma("sync", rrep[:], rsq_dram[0:1, row * AW:(row + 1) * AW].broadcast_to([128, AW]), "dx")
                    P.wait("vector", "dx", P.cnt["dx"])
                    for ci in range(2):
                        kk = hh * 2 + ci
                        P.op("vector", lambda En, fin=fin, kk=kk: En.tensor_mul(
                            fin[:, kk, 0:AW], fin[:, kk, 0:AW], rrep[:]), inc="v")
                    P.wait("sync", "v", P.cnt["v"])
            for hh in range(2):
                P.dma("sync", brep[:], beta_dram[hh:hh + 1, :].broadcast_to([128, AW]), "dx")
                P.wait("vector", "dx", P.cnt["dx"])
                for ci in range(2):
                    kk = hh * 2 + ci
                    P.op("vector", lambda En, kk=kk: En.tensor_mul(
                        kb_fm[:, kk, 0:AW], k_fm[:, kk, 0:AW], brep[:]), inc="v")
                P.wait("sync", "v", P.cnt["v"])
            norm_done = P.cnt["v"]

            P.wait("scalar", "v", norm_done)
            for kk in range(4):
                P.tdma(k_tm[:, :, kk, :],
                       k_fm[:, kk, 0:AW].rearrange("p (t c) -> p t c", t=NSW))
            for kk in range(4):
                P.tdma(v_tm[:, 4 * w:4 * w + 4, kk * 128:(kk + 1) * 128],
                       v_fm[:, kk, 6:6 + AW].rearrange("p (t c) -> p t c", t=NSW))
            P.tdma(beta_tm[:, :, :], beta_bf[:, :].rearrange("p (t c) -> p t c", t=NSW))
            tp_done = P.cnt["ds"]
            P.wait("vector", "ds", tp_done)
            P.wait("tensor", "ds", tp_done)
            P.op("vector", lambda En: En.tensor_copy(beta_tmf[:], beta_tm[:, :, 0:2]), inc="v")
            P.wait("vector", "v")

            for hc in range(16):
                buf = wf1blk[hc % 2]
                if hc >= 2:
                    P.wait("sync", "t", t_h1[hc - 2])
                dv = P.dma("sync", buf[:],
                           wf1bf.rearrange("(k p) o -> p k o", p=128)[:, :, hc * 128:(hc + 1) * 128],
                           "dl")
                P.wait("tensor", "dl", dv)
                for k in range(8):
                    P.op("tensor", lambda En, k=k, buf=buf: En.matmul(
                        pb[4][:, :], buf[:, k, :], xn_fm[:, k, :],
                        start=(k == 0), stop=(k == 7)), inc=("t" if k == 7 else None))
                if hc == 0:
                    t_h1 = {}
                t_h1[hc] = P.cnt["t"]
                P.wait("scalar", "t", t_h1[hc])
                P.op("scalar", lambda En, hc=hc: En.activation(
                    h1blk[:, :], pb[4][:, :], Act.Gelu, bias=bf1_sb[:, hc:hc + 1]), inc="s")
                P.wait("tensor", "s")
                for lt in range(NSW):
                    P.op("tensor", lambda En, hc=hc, lt=lt: En.matmul(
                        pb[5][:, lt * 8:(lt + 1) * 8],
                        h1blk[:, lt * 128:(lt + 1) * 128], wf2_sb[:, hc, :],
                        start=(hc == 0 and lt == 0), stop=False), inc=None)
                P.op("tensor", lambda En: En.nop(), inc="t")
                t_h1[hc] = P.cnt["t"]
            for lt in range(NSW):
                P.op("tensor", lambda En, lt=lt: En.matmul(
                    pb[5][:, lt * 8:(lt + 1) * 8], onesrow_sb[:, :], bf2_sb[:, :],
                    start=False, stop=True), inc="t")
            tdone = P.cnt["t"]
            P.wait("vector", "t", tdone)
            for lt in range(NSW):
                P.op("vector", lambda En, lt=lt: En.tensor_copy(
                    lgt_s[:], pb[5][:, lt * 8:(lt + 1) * 8]), inc="v")
                P.wait("vector", "v")
                P.op("vector", lambda En: En.reduce_max(
                    fmax[:], lgt_s[:].rearrange("p (h j) -> p h j", j=4), axis=AX), inc="v")
                P.wait("vector", "v")
                P.op("vector", lambda En: En.tensor_sub(
                    fexp[:].rearrange("p (h j) -> p h j", j=4),
                    lgt_s[:].rearrange("p (h j) -> p h j", j=4),
                    fmax[:].unsqueeze(2).broadcast_to([128, 2, 4])), inc="v")
                P.wait("scalar", "v")
                P.op("scalar", lambda En: En.activation(fexp[:], fexp[:], Act.Exp), inc="s")
                P.wait("vector", "s")
                P.op("vector", lambda En: En.reduce_sum(
                    fsum[:], fexp[:].rearrange("p (h j) -> p h j", j=4), axis=AX), inc="v")
                P.wait("vector", "v")
                P.op("vector", lambda En: En.reciprocal(fmax[:], fsum[:]), inc="v")
                P.wait("vector", "v")
                P.op("vector", lambda En, w=w, lt=lt: En.tensor_mul(
                    fw_tm[:, 4 * w + lt, :].rearrange("p (h j) -> p h j", j=4),
                    fexp[:].rearrange("p (h j) -> p h j", j=4),
                    fmax[:].unsqueeze(2).broadcast_to([128, 2, 4])), inc="v")
            P.wait("tensor", "v")

            P.wait("gpsimd", "v", P.cnt["v"])
            for kk in range(4):
                eng = "vector" if kk % 2 == 0 else "gpsimd"
                sem = "v" if kk % 2 == 0 else "g"
                s0, s1 = (cA, cB) if kk % 2 == 0 else (cC, cD)
                bufs = [s0, s1, s0, s1, s0, s1, s0]
                P.op(eng, lambda En, kk=kk, s0=s0: En.tensor_scalar_mul(
                    s0[:, :], v_fm[:, kk, 0:AW], w7_sb[:, kk, 0:1]), inc=sem)
                P.wait(eng, sem)
                for tap in range(1, 7):
                    src_b, dst_b = bufs[tap - 1], bufs[tap]
                    if eng == "vector":
                        P.op(eng, lambda En, kk=kk, tap=tap, src_b=src_b, dst_b=dst_b:
                             En.scalar_tensor_tensor(
                                 dst_b[:, :], v_fm[:, kk, tap:tap + AW],
                                 w7_sb[:, kk, tap:tap + 1], src_b[:, :],
                                 Alu.mult, Alu.add), inc=sem)
                        P.wait(eng, sem)
                    else:
                        P.op(eng, lambda En, kk=kk, tap=tap: En.tensor_scalar_mul(
                            cE[:, :], v_fm[:, kk, tap:tap + AW],
                            w7_sb[:, kk, tap:tap + 1]), inc=sem)
                        P.wait(eng, sem)
                        P.op(eng, lambda En, src_b=src_b, dst_b=dst_b: En.tensor_add(
                            dst_b[:, :], src_b[:, :], cE[:, :]), inc=sem)
                        P.wait(eng, sem)
                P.op(eng, lambda En, kk=kk, fb=bufs[6]: En.tensor_copy(
                    short_fm[:, kk, :], fb[:, :]), inc=sem)
            P.wait("scalar", "v", P.cnt["v"])
            P.wait("scalar", "g", P.cnt["g"])
            for kk in range(4):
                P.tdma(short_tm[:, :, kk, :],
                       short_fm[:, kk, :].rearrange("p (t c) -> p t c", t=NSW))
            stp = P.cnt["ds"]
            P.wait("vector", "ds", stp)
            P.wait("gpsimd", "ds", stp)
            P.wait("tensor", "ds", stp)

            for h in range(2):
                ck0, ck1 = 2 * h, 2 * h + 1
                for sw in range(NSW):
                    c0 = sw * 128
                    qs = [q_fm[:, ck0, c0:c0 + 128], q_fm[:, ck1, c0:c0 + 128]]
                    ks = [k_fm[:, ck0, c0:c0 + 128], k_fm[:, ck1, c0:c0 + 128]]
                    kbs = [kb_fm[:, ck0, c0:c0 + 128], kb_fm[:, ck1, c0:c0 + 128]]
                    ktm = [k_tm[:, sw, ck0, :], k_tm[:, sw, ck1, :]]
                    sbf = [S_bf[:, h, 0, :], S_bf[:, h, 1, :]]

                    P.wait("vector", "t")
                    P.op("vector", lambda En, sw=sw, ck0=ck0, h=h: En.tensor_scalar_mul(
                        kb_tm[:, 0, :], k_tm[:, sw, ck0, :], beta_tmf[:, sw, h:h + 1]), inc="v")
                    P.op("vector", lambda En, sw=sw, ck1=ck1, h=h: En.tensor_scalar_mul(
                        kb_tm[:, 1, :], k_tm[:, sw, ck1, :], beta_tmf[:, sw, h:h + 1]), inc="v")
                    P.op("vector", lambda En, sw=sw, w=w, h=h: En.tensor_scalar_mul(
                        vb_tm[:], v_tm[:, 4 * w + sw, h * 256:(h + 1) * 256],
                        beta_tmf[:, sw, h:h + 1]), inc="v")
                    P.wait("tensor", "v")

                    def g2(bank, l0, r0, l1, r1):
                        P.op("tensor", lambda En, bank=bank, l0=l0, r0=r0: En.matmul(
                            bank, l0, r0, start=True, stop=False))
                        P.op("tensor", lambda En, bank=bank, l1=l1, r1=r1: En.matmul(
                            bank, l1, r1, start=False, stop=True), inc="t")

                    def g1(bank, l0, r0):
                        P.op("tensor", lambda En, bank=bank, l0=l0, r0=r0: En.matmul(
                            bank, l0, r0, start=True, stop=True), inc="t")

                    g2(pb[0][:, 0:128], kbs[0], ks[0], kbs[1], ks[1])
                    evac(dt["X"][:], pb[0][:, 0:128], mask=mbd_sb)
                    g2(pb[0][:, 0:128], ks[0], kbs[0], ks[1], kbs[1])
                    evac(dt["Y"][:], pb[0][:, 0:128], mask=mbdT_sb)
                    g2(pb[0][:, 0:128], ks[0], qs[0], ks[1], qs[1])
                    evac(dt["Ct"][:], pb[0][:, 0:128], mask=mtriu_sb)
                    for (xa, ya, xo, yo) in [("X", "Y", "X2", "Y2"),
                                             ("X2", "Y2", "X4", "Y4"),
                                             ("X4", "Y4", "X8", "Y8"),
                                             ("X8", "Y8", "X16", "Y16")]:
                        g1(pb[0][:, 0:128], dt[ya][:], dt[xa][:])
                        evac(dt[xo][:], pb[0][:, 0:128])
                        g1(pb[0][:, 0:128], dt[xa][:], dt[ya][:])
                        evac(dt[yo][:], pb[0][:, 0:128])
                    P.wait("vector", "t")
                    P.op("vector", lambda En: En.tensor_add(dt["Pm"][:], dt["Y"][:],
                                                            id_sb[:]), inc="v")
                    P.wait("tensor", "v")
                    for m in ["X2", "X4", "X8", "X16"]:
                        P.op("tensor", lambda En, m=m: En.matmul(
                            pb[0][:, 0:128], dt[m][:], dt["Pm"][:], start=True, stop=False))
                        P.op("tensor", lambda En: En.matmul(
                            pb[0][:, 0:128], id_sb[:], dt["Pm"][:], start=False, stop=True),
                            inc="t")
                        evac(dt["Pm"][:], pb[0][:, 0:128])
                    g1(pb[1][:, 0:256], dt["Pm"][:], vb_tm[:])
                    evac(dU[:], pb[1][:, 0:256])
                    g1(pb[0][:, 0:128], kb_tm[:, 0, :], dt["Pm"][:])
                    evac(wt_sb[:, 0, :], pb[0][:, 0:128])
                    g1(pb[0][:, 0:128], kb_tm[:, 1, :], dt["Pm"][:])
                    evac(wt_sb[:, 1, :], pb[0][:, 0:128])
                    g2(pb[1][:, 0:256], wt_sb[:, 0, :], sbf[0], wt_sb[:, 1, :], sbf[1])
                    P.wait("vector", "t")
                    P.op("vector", lambda En: En.tensor_sub(dRHS[:], dU[:],
                                                            pb[1][:, 0:256]), inc="v")
                    P.wait("tensor", "v")
                    if debug and w == dbg_w and h == 0 and sw == 1:
                        P.op("vector", lambda En: En.tensor_copy(mt0[:], sbf[0]), inc="v")
                        P.wait("gpsimd", "v", P.cnt["v"])
                        P.dma("gpsimd", dbg["dbg_p1"], mt0[:], "dg")
                        P.wait("vector", "dg", P.cnt["dg"])
                    g2(pb[0][:, 0:128], wt_sb[:, 0, :], ks[0], wt_sb[:, 1, :], ks[1])
                    evac(dt["Nn"][:], pb[0][:, 0:128], mask=mblk_sb)
                    g2(pb[0][:, 0:128], ks[0], wt_sb[:, 0, :], ks[1], wt_sb[:, 1, :])
                    evac(dt["Nt"][:], pb[0][:, 0:128], mask=mblkT_sb)
                    g1(pb[0][:, 0:128], dt["Nt"][:], dt["Nn"][:])
                    evac(dt["N2"][:], pb[0][:, 0:128])
                    g1(pb[0][:, 0:128], dt["Nn"][:], dt["Nt"][:])
                    evac(dt["N2t"][:], pb[0][:, 0:128])
                    if debug and w == dbg_w and h == 0 and sw == 1:
                        P.op("vector", lambda En: En.tensor_copy(mt0[:], sbf[0]), inc="v")
                        P.wait("gpsimd", "v", P.cnt["v"])
                        P.dma("gpsimd", dbg["dbg_p2"], mt0[:], "dg")
                        P.wait("vector", "dg", P.cnt["dg"])
                    P.op("tensor", lambda En: En.matmul(pb[1][:, 0:256], dt["N2t"][:],
                                                        dRHS[:], start=True, stop=False))
                    P.op("tensor", lambda En: En.matmul(pb[1][:, 0:256], id_sb[:],
                                                        dRHS[:], start=False, stop=True), inc="t")
                    evac(dT1[:], pb[1][:, 0:256])
                    if debug and w == dbg_w and h == 0 and sw == 1:
                        P.op("vector", lambda En: En.tensor_copy(mt0[:], sbf[0]), inc="v")
                        P.wait("gpsimd", "v", P.cnt["v"])
                        P.dma("gpsimd", dbg["dbg_p3"], mt0[:], "dg")
                        P.wait("vector", "dg", P.cnt["dg"])
                        P.wait("tensor", "dg", P.cnt["dg"])
                    P.op("tensor", lambda En: En.matmul(pb[1][:, 0:256], dt["Nt"][:],
                                                        dT1[:], start=True, stop=False))
                    P.op("tensor", lambda En: En.matmul(pb[1][:, 0:256], id_sb[:],
                                                        dT1[:], start=False, stop=True), inc="t")
                    evac(dUt[:], pb[1][:, 0:256])
                    if False and debug and w == dbg_w and h == 0 and sw == 1:
                        P.wait("vector", "t")
                        P.op("vector", lambda En: En.tensor_copy(kb_tm[:, 0, :], qs[0]), inc="v")
                        P.op("vector", lambda En: En.tensor_copy(kb_tm[:, 1, :], qs[1]), inc="v")
                        P.op("vector", lambda En: En.tensor_copy(mt1[:], sbf[0]), inc="v")
                        P.op("vector", lambda En: En.tensor_copy(mt1[:], sbf[1]), inc="v")
                        P.wait("tensor", "v")
                        P.op("tensor", lambda En: En.matmul(pb[3][:, 0:256], kb_tm[:, 0, :], sbf[0],
                                                            start=True, stop=False))
                        P.op("tensor", lambda En: En.matmul(pb[3][:, 0:256], kb_tm[:, 1, :], sbf[1],
                                                            start=False, stop=True), inc="t")
                        P.wait("vector", "t")
                        P.op("vector", lambda En: En.tensor_copy(mt0[:], pb[3][:, 0:256]), inc="v")
                        P.wait("gpsimd", "v", P.cnt["v"])
                        P.dma("gpsimd", dbg["dbg_QS"], mt0[:], "dg")
                        P.dma("gpsimd", dbg["dbg_sbf"], mt1[:], "dg")
                        P.dma("gpsimd", dbg["dbg_qcopy"], kb_tm[:], "dg")
                        P.wait("tensor", "dg", P.cnt["dg"])
                        P.wait("vector", "dg", P.cnt["dg"])
                    P.op("tensor", lambda En, a=qs[0], b_=sbf[0]: En.matmul(
                        pb[1][:, 0:256], a, b_, start=True, stop=False))
                    P.op("tensor", lambda En, a=qs[1], b_=sbf[1]: En.matmul(
                        pb[1][:, 0:256], a, b_, start=False, stop=False))
                    P.op("tensor", lambda En: En.matmul(pb[1][:, 0:256], dt["Ct"][:],
                                                        dUt[:], start=False, stop=True), inc="t")
                    evac(delta_tm[:, sw, h * 256:(h + 1) * 256], pb[1][:, 0:256])
                    for ci in range(2):
                        g1(pb[2][:, 0:256], ktm[ci], dUt[:])
                        P.wait("vector", "t")
                        P.op("vector", lambda En, h=h, ci=ci: En.tensor_add(
                            S_f[:, h, ci, :], S_f[:, h, ci, :], pb[2][:, 0:256]), inc="v")
                        P.wait("tensor", "v")
                    P.op("vector", lambda En, h=h: En.tensor_copy(
                        S_bf[:, h, :, :], S_f[:, h, :, :]), inc="v")
                    P.wait("tensor", "v")
                    if debug and w == dbg_w and h == 0 and sw == 1:
                        P.wait("gpsimd", "v", P.cnt["v"])
                        P.dma("gpsimd", dbg["dbg_RHS"], dRHS[:], "dg")
                        P.dma("gpsimd", dbg["dbg_Ut1"], dUt[:], "dg")
                        P.dma("gpsimd", dbg["dbg_N1"], dt["Nn"][:], "dg")
                        P.dma("gpsimd", dbg["dbg_T11"], dT1[:], "dg")
                        P.wait("vector", "dg", P.cnt["dg"])
                        P.wait("scalar", "dg", P.cnt["dg"])
                        P.wait("tensor", "dg", P.cnt["dg"])
                    if debug and w == dbg_w and h == 0 and sw == 0:
                        P.wait("gpsimd", "v", P.cnt["v"])
                        P.dma("gpsimd", dbg["dbg_S"], S_bf[:], "dg")
                        P.wait("vector", "dg", P.cnt["dg"])
                        P.wait("gpsimd", "v", P.cnt["v"])
                        for nm_, t_ in [("dbg_X", dt["X"]), ("dbg_Y", dt["Y"]),
                                        ("dbg_Pm", dt["Pm"]), ("dbg_U", dU),
                                        ("dbg_Nn", dt["Nn"]), ("dbg_Ut", dUt),
                                        ("dbg_Ct", dt["Ct"])]:
                            P.dma("gpsimd", dbg[nm_], t_[:], "dg")
                        P.dma("gpsimd", dbg["dbg_Wt"], wt_sb[:], "dg")
                        P.dma("gpsimd", dbg["dbg_beta"], beta_tmf[:], "dg")
                        P.wait("vector", "dg", P.cnt["dg"])
                        P.wait("scalar", "dg", P.cnt["dg"])
                        P.wait("tensor", "dg", P.cnt["dg"])

            P.wait("vector", "v")
            for lt in range(NSW):
                for h in range(2):
                    fwc = fw_tm[:, 4 * w + lt, :]
                    P.op("vector", lambda En, lt=lt, h=h, fwc=fwc: En.tensor_scalar_mul(
                        mt0[:].rearrange("p (a c) -> p a c", a=2),
                        short_tm[:, lt, 2 * h:2 * h + 2, :],
                        fwc[:, 4 * h:4 * h + 1]), inc="v")
                    P.wait("vector", "v")
                    P.op("vector", lambda En, lt=lt, h=h, fwc=fwc: En.scalar_tensor_tensor(
                        mt1[:], delta_tm[:, lt, h * 256:(h + 1) * 256],
                        fwc[:, 4 * h + 2:4 * h + 3], mt0[:],
                        Alu.mult, Alu.add), inc="v")
                    P.wait("vector", "v")
                    P.op("vector", lambda En, lt=lt, h=h, w=w, fwc=fwc: En.scalar_tensor_tensor(
                        acc_tm[:, 4 * w + lt, h * 256:(h + 1) * 256],
                        v_tm[:, 4 * w + lt, h * 256:(h + 1) * 256],
                        fwc[:, 4 * h + 3:4 * h + 4], mt1[:],
                        Alu.mult, Alu.add), inc="v")
                    P.wait("vector", "v")
            if debug and w == dbg_w:
                P.wait("gpsimd", "v", P.cnt["v"])
                P.wait("gpsimd", "s", P.cnt["s"])
                P.dma("gpsimd", dbg["dbg_q"], q_fm[:], "dg")
                P.dma("gpsimd", dbg["dbg_k"], k_fm[:], "dg")
                P.dma("gpsimd", dbg["dbg_v"], v_fm[:], "dg")
                P.dma("gpsimd", dbg["dbg_delta"], delta_tm[:], "dg")
                P.dma("gpsimd", dbg["dbg_short"], short_tm[:], "dg")
                P.wait("vector", "dg", P.cnt["dg"])
                P.wait("scalar", "dg", P.cnt["dg"])

        if debug:
            P.wait("gpsimd", "v", P.cnt["v"])
            P.dma("gpsimd", dbg["dbg_fw"], fw_tm[:], "dg")
            P.dma("gpsimd", dbg["dbg_acc"], acc_tm[:], "dg")
            P.wait("vector", "dg", P.cnt["dg"])
            P.wait("tensor", "dg", P.cnt["dg"])

        t_fir = {}
        for h in range(2):
            for cg in range(16):
                bank = pb[cg % 2]
                for ci in range(16):
                    ch = h * 256 + cg * 16 + ci
                    bi = ci % 2
                    if (cg * 16 + ci) >= 2 or h > 0:
                        key = (h, cg * 16 + ci - 2) if cg * 16 + ci >= 2 else (h - 1, 254)
                        P.wait("gpsimd", "t", t_fir.get(key, 0))
                    P.dma("gpsimd", tlc_sb[bi][:], tlc[ch], "dg")
                    P.dma("gpsimd", tlp_sb[bi][64:128, :], tlp[ch], "dg")
                    dgv = P.cnt["dg"]
                    P.wait("tensor", "dg", dgv)
                    P.op("tensor", lambda En, bank=bank, bi=bi, ci=ci, ch=ch: En.matmul(
                        bank[:, ci * 32:(ci + 1) * 32], tlc_sb[bi][:, :],
                        v_tm[:, :, ch].unsqueeze(2).rearrange("p t a -> p (t a)")
                        if False else v_tm[:, :, ch],
                        start=(ci == 0), stop=False))
                    P.op("tensor", lambda En, bank=bank, bi=bi, ci=ci, ch=ch: En.matmul(
                        bank[:, ci * 32 + 1:ci * 32 + 32], tlp_sb[bi][64:128, :],
                        v_tm[64:128, 0:31, ch],
                        start=False, stop=(ci == 15)),
                        inc=("t" if ci == 15 else None))
                    t_fir[(h, cg * 16 + ci)] = P.cnt["t"]
                P.wait("vector", "t")
                P.op("vector", lambda En, bank=bank, cg=cg: En.tensor_copy(
                    long_h[:, :, cg * 16:(cg + 1) * 16],
                    bank[:, :].rearrange("p (a b) -> p b a", a=16)), inc="v")
                P.wait("tensor", "v")
            P.wait("vector", "v")
            for wt in range(32):
                P.op("vector", lambda En, wt=wt, h=h: En.scalar_tensor_tensor(
                    mt0[:], long_h[:, wt, :], fw_tm[:, wt, 4 * h + 1:4 * h + 2],
                    acc_tm[:, wt, h * 256:(h + 1) * 256], Alu.mult, Alu.add), inc="v")
                P.wait("scalar", "v")
                P.op("scalar", lambda En: En.activation(sqd[:], mt0[:], Act.Square,
                                                        accum_out=rmss[:]), inc="s")
                P.wait("vector", "s")
                P.op("vector", lambda En: En.tensor_scalar(
                    rmss[:], rmss[:], 1.0 / 256.0, LN_EPS, Alu.mult, Alu.add), inc="v")
                P.wait("scalar", "v")
                P.op("scalar", lambda En: En.activation(rmss[:], rmss[:], Act.Sqrt), inc="s")
                P.wait("vector", "s")
                P.op("vector", lambda En: En.reciprocal(rmsr[:], rmss[:]), inc="v")
                P.wait("vector", "v")
                P.op("vector", lambda En: En.tensor_scalar_mul(of32[:], mt0[:],
                                                               rmsr[:]), inc="v")
                P.wait("sync", "v", P.cnt["v"])
                P.dma("sync", o_out[wt * 128:(wt + 1) * 128, h * 256:(h + 1) * 256],
                      of32[:], "do")
                P.wait("vector", "do", P.cnt["do"])
        P.wait("sync", "do", P.cnt["do"])
        P.finish()
    return nc


def _toeplitz_host(w_half):
    nch = w_half.shape[0]
    j = np.arange(64)[:, None]
    m = np.arange(128)[None, :]
    idxp = j - m - 1
    okp = (idxp >= 0) & (idxp < 64)
    i_ = np.arange(128)[:, None]
    idxc = i_ - m + 63
    okc = (idxc >= 0) & (idxc < 64)
    TLp = np.zeros((nch, 64, 128), np.float32)
    TLc = np.zeros((nch, 128, 128), np.float32)
    ip = np.clip(idxp, 0, 63)
    ic = np.clip(idxc, 0, 63)
    for ch in range(nch):
        TLp[ch] = np.where(okp, w_half[ch][ip], 0.0)
        TLc[ch] = np.where(okc, w_half[ch][ic], 0.0)
    return TLp, TLc


def run_A(inputs, debug=False, dbg_w=NW - 1):
    x = _to_f32c(inputs["L0_attn_hidden_states"])
    wq = _to_f32c(inputs["L0_attn_Wq"]); wk = _to_f32c(inputs["L0_attn_Wk"])
    wv = _to_f32c(inputs["L0_attn_Wv"]); wb = _to_f32c(inputs["L0_attn_Wb"])
    cq = _to_f32c(inputs["L0_attn_conv_q_w"]); ck = _to_f32c(inputs["L0_attn_conv_k_w"])
    cv = _to_f32c(inputs["L0_attn_conv_v_w"])
    w7 = _to_f32c(inputs["L0_attn_fir_short_w"]).reshape(1024, 7)
    w64 = _to_f32c(inputs["L0_attn_fir_long_w"]).reshape(1024, 64)
    wf1 = _to_f32c(inputs["L0_attn_Wf1"]); bf1 = _to_f32c(inputs["L0_attn_bf1"])
    wf2 = _to_f32c(inputs["L0_attn_Wf2"]); bf2 = _to_f32c(inputs["L0_attn_bf2"])
    tl = {}
    for hp in range(2):
        tl[hp] = _toeplitz_host(w64[512 * hp:512 * hp + 512])
    nc = build_A(debug=debug, dbg_w=dbg_w)
    in_maps = []
    for c in range(NCORE):
        b, hp = c // 2, c % 2
        cols = slice(512 * hp, 512 * hp + 512)

        def chw(a):
            return np.ascontiguousarray(
                a[cols].reshape(4, 128, -1).transpose(1, 0, 2))
        in_maps.append({
            "xb": np.ascontiguousarray(x[b]),
            "wq_s": np.ascontiguousarray(wq[:, cols]),
            "wk_s": np.ascontiguousarray(wk[:, cols]),
            "wv_s": np.ascontiguousarray(wv[:, cols]),
            "wb_s": np.ascontiguousarray(wb[:, [2 * hp, 2 * hp + 1]]),
            "cq_s": chw(cq), "ck_s": chw(ck), "cv_s": chw(cv),
            "w7_s": chw(w7),
            "wf1": wf1,
            "bf1_pt": np.ascontiguousarray(bf1.reshape(16, 128).T),
            "wf2_s": np.ascontiguousarray(wf2[:, 8 * hp:8 * hp + 8]),
            "bf2_s": bf2[8 * hp:8 * hp + 8][None, :],
            "tlp": tl[hp][0], "tlc": tl[hp][1],
        })
    res = run_bass_kernel_spmd(nc, in_maps, core_ids=list(range(NCORE)))
    if debug:
        return res
    o_full = np.zeros((B, L, D), np.float32)
    for c in range(NCORE):
        b, hp = c // 2, c % 2
        o_full[b, :, 512 * hp:512 * hp + 512] = res.results[c]["o_slice"]
    return o_full



def _to_f32c(a):
    return np.ascontiguousarray(np.asarray(a, dtype=np.float32))


def run_B(inputs):
    x = _to_f32c(inputs["L0_attn_hidden_states"])
    wqkvT = _to_f32c(np.asarray(inputs["L0_ffn_in_proj_w"]).T)
    woT = _to_f32c(np.asarray(inputs["L0_ffn_out_proj_w"]).T)
    nc = build_B()
    in_maps = []
    for c in range(NCORE):
        xs = _to_f32c(x[:, c * SL:(c + 1) * SL, :].reshape(4 * SL, D))
        in_maps.append({
            "xs": xs,
            "wqkvT": wqkvT,
            "bqkv": _to_f32c(inputs["L0_ffn_in_proj_b"])[None, :],
            "woT": woT,
            "bo": _to_f32c(inputs["L0_ffn_out_proj_b"])[None, :],
            "ln1g": _to_f32c(inputs["L0_ffn_ln1_g"])[None, :],
            "ln1b": _to_f32c(inputs["L0_ffn_ln1_b"])[None, :],
            "wg": _to_f32c(inputs["L0_ffn_wg"]),
        })
    res = run_bass_kernel_spmd(nc, in_maps, core_ids=list(range(NCORE)))
    out = []
    for c in range(NCORE):
        r = res.results[c]
        out.append({
            "y1": r["y1"],
            "idx": r["idx"].T.reshape(-1),
            "gv": r["gv"].T.reshape(-1),
            "lr": r["lr"].T.reshape(-1),
            "cnt": r["cnt"].reshape(4, E),
            "idx_pt": r["idx"], "gv_pt": r["gv"], "lr_pt": r["lr"],
        })
    return out


POSC = (np.arange(NT)[None, :] * 128 + np.arange(128)[:, None]).astype(np.float32)
run_C_debug = [False]


def _msel(c):
    m = np.zeros((32, 4), np.float32)
    for b in range(4):
        m[c * 4 + b, b] = 1.0
    return m


def run_C(inputs, bouts):
    y1all = np.concatenate([bo["y1"] for bo in bouts], 0)
    cnt_all = np.stack([bo["cnt"] for bo in bouts]).reshape(32, 8).astype(np.float32)
    w1 = _to_f32c(inputs["L0_ffn_w1"])
    w2 = _to_f32c(inputs["L0_ffn_w2"])
    b1 = _to_f32c(inputs["L0_ffn_b1"])
    b2 = _to_f32c(inputs["L0_ffn_b2"])
    nc = build_C(debug=run_C_debug[0])
    in_maps = []
    for c in range(NCORE):
        in_maps.append({
            "y1all": y1all,
            "cnt_all": cnt_all,
            "msel": _msel(c),
            "idx_pt": bouts[c]["idx_pt"],
            "lr_pt": bouts[c]["lr_pt"],
            "posc": POSC,
            "coreoff": np.full((128, 1), c * 2048, np.float32),
            "w1e": w1[c],
            "b1e": np.ascontiguousarray(b1[c].reshape(32, 128).T),
            "w2e": w2[c],
            "b2e": b2[c][None, :],
        })
    res = run_bass_kernel_spmd(nc, in_maps, core_ids=list(range(NCORE)))
    if run_C_debug[0]:
        return res
    return [res.results[c]["Y"] for c in range(NCORE)]


def run_D(inputs, bouts, Ys, o_full):
    x = _to_f32c(inputs["L0_attn_hidden_states"])
    yall = np.concatenate(Ys, 0)
    cnt_all = np.stack([bo["cnt"] for bo in bouts]).reshape(32, 8).astype(np.float32)
    rmsw_t = np.tile(_to_f32c(inputs["L0_attn_rms_w"]), NH)
    rmsw = np.ascontiguousarray(rmsw_t.reshape(8, 128).T)
    nc = build_D()
    in_maps = []
    for c in range(NCORE):
        xs = _to_f32c(x[:, c * SL:(c + 1) * SL, :].reshape(4 * SL, D))
        o_fm = np.ascontiguousarray(
            o_full[:, c * SL:(c + 1) * SL, :].reshape(4 * SL, D).T)
        in_maps.append({
            "xs": xs,
            "y1s": bouts[c]["y1"],
            "yall": yall,
            "o_fm": o_fm,
            "wo": _to_f32c(inputs["L0_attn_Wo"]),
            "rmsw": rmsw,
            "ln2g": _to_f32c(inputs["L0_ffn_ln2_g"])[None],
            "ln2b": _to_f32c(inputs["L0_ffn_ln2_b"])[None],
            "gv_pt": bouts[c]["gv_pt"],
            "cnt_all": cnt_all,
            "msel": _msel(c),
            "idx_pt": bouts[c]["idx_pt"],
            "lr_pt": bouts[c]["lr_pt"],
            "posc": POSC,
            "coreoff": np.full((128, 1), c * 2048, np.float32),
        })
    res = run_bass_kernel_spmd(nc, in_maps, core_ids=list(range(NCORE)))
    out = np.zeros((B, L, D), np.float32)
    for c in range(NCORE):
        out[:, c * SL:(c + 1) * SL, :] = res.results[c]["out"].reshape(4, SL, D)
    return out


def kernel(**inputs):
    bouts = run_B(inputs)
    o_full = run_A(inputs)
    Ys = run_C(inputs, bouts)
    out = run_D(inputs, bouts, Ys, o_full)
    return out
